# version 25
# baseline (speedup 1.0000x reference)
import numpy as np
import concourse.bass as bass
import concourse.mybir as mybir
from concourse.bass_utils import run_bass_kernel_spmd
from contextlib import ExitStack

F32 = mybir.dt.float32
BF16 = mybir.dt.bfloat16
AF = mybir.ActivationFunctionType
ALU = mybir.AluOpType
AX = mybir.AxisListType

S = 2048
D = 1024
DI = 2816
NT = S // 128
ND = D // 128
NI = DI // 128
MAGIC = 12582912.0
EPS = 1e-5
QMAX = 127.0
INV127 = 1.0 / 127.0
SCALE_QK = 1.0 / 32.0


class B:

    ENGS = ("pe", "act", "dve", "pool", "sp")
    NQ = {"qsp": 12, "qgp": 8}

    def __init__(self):
        self.ops = {e: [] for e in B.ENGS}
        self.cnt = {e: 0 for e in B.ENGS}
        self.qidx = {q: 0 for q in B.NQ}
        self.qsemcnt = {f"{q}{i}": 0 for q in B.NQ for i in range(B.NQ[q])}
        self.sem = {}
        self.writer = {}
        self.readers = {}
        self.waited = {e: {} for e in B.ENGS}

    def _deps(self, reads, writes, extra):
        deps = []
        for r in reads:
            w = self.writer.get(r)
            if w:
                deps.append(w)
        for w in writes:
            ww = self.writer.get(w)
            if ww:
                deps.append(ww)
            deps.extend(self.readers.get(w, ()))
        deps.extend(extra)
        return deps

    def _emit(self, eng, fn, deps, inc_sem, tok):
        seq = []
        for (sname, val) in deps:
            if self.waited[eng].get(sname, 0) >= val:
                continue
            seq.append((sname, val))
            self.waited[eng][sname] = val
        sems = self.sem
        inc_val = 16 if inc_sem.startswith("q") else 1
        assert not (inc_sem.startswith("q") and inc_val != 16)

        def closure(e, seq=seq, fn=fn, inc_sem=inc_sem, inc_val=inc_val):
            for (sname, val) in seq:
                e.wait_ge(sems[sname], val)
            ins = fn(e)
            if ins is not None:
                ins.then_inc(sems[inc_sem], inc_val)
        self.ops[eng].append(closure)
        return tok

    def op(self, eng, fn, reads=(), writes=(), accum=(), extra=(), noinc=False,
           reg_reads=()):
        if noinc:
            def closure(e, fn=fn):
                fn(e)
            self.ops[eng].append(closure)
            return None
        deps = self._deps(reads, writes, extra)
        self.cnt[eng] += 1
        tok = (eng, self.cnt[eng])
        for w in writes:
            self.writer[w] = tok
            self.readers[w] = []
        for a in accum:
            self.writer[a] = tok
        for r in reads:
            self.readers.setdefault(r, []).append(tok)
        for r in reg_reads:
            self.readers.setdefault(r, []).append(tok)
        return self._emit(eng, fn, deps, eng, tok)

    def gdeps(self, keys):
        out = []
        for k in keys:
            w = self.writer.get(k)
            if w:
                out.append(w)
        return out

    def dma(self, queue, fn, reads=(), writes=(), extra=()):
        eng = "sp" if queue == "qsp" else "pool"
        deps = self._deps(reads, writes, extra)
        sname = f"{queue}{self.qidx[queue] % B.NQ[queue]}"
        self.qidx[queue] += 1
        if self.qsemcnt[sname] > 0:
            deps = deps + [(sname, self.qsemcnt[sname])]
        self.qsemcnt[sname] += 16
        tok = (sname, self.qsemcnt[sname])
        for w in writes:
            self.writer[w] = tok
            self.readers[w] = []
        for r in reads:
            self.readers.setdefault(r, []).append(tok)
        return self._emit(eng, fn, deps, sname, tok)

    def snap(self):
        toks = [(e, self.cnt[e]) for e in B.ENGS if self.cnt[e] > 0]
        toks += [(s, c) for s, c in self.qsemcnt.items() if c > 0]
        return toks

    def run(self, nc):
        with ExitStack() as ctx:
            for e in B.ENGS:
                self.sem[e] = ctx.enter_context(nc.semaphore(f"sem_{e}"))
            for s in self.qsemcnt:
                self.sem[s] = ctx.enter_context(nc.semaphore(f"sem_{s}"))
            block = ctx.enter_context(nc.Block())
            prog = self

            @block.tensor
            def _(t):
                for f in prog.ops["pe"]:
                    f(t)

            @block.scalar
            def _(s):
                for f in prog.ops["act"]:
                    f(s)

            @block.vector
            def _(v):
                for f in prog.ops["dve"]:
                    f(v)

            @block.gpsimd
            def _(g):
                for f in prog.ops["pool"]:
                    f(g)

            @block.sync
            def _(sy):
                for f in prog.ops["sp"]:
                    f(sy)
                for s, c in prog.qsemcnt.items():
                    if c > 0:
                        sy.wait_ge(prog.sem[s], c)


def build_nc():
    nc = bass.Bass("TRN2", target_bir_lowering=False)

    x_d = nc.dram_tensor("x", [S, D], F32, kind="ExternalInput")
    wq_d = nc.dram_tensor("wq", [D, D], F32, kind="ExternalInput")
    wk_d = nc.dram_tensor("wk", [D, D], F32, kind="ExternalInput")
    wv_d = nc.dram_tensor("wv", [D, D], F32, kind="ExternalInput")
    bq_d = nc.dram_tensor("bq", [1, D], F32, kind="ExternalInput")
    bk_d = nc.dram_tensor("bk", [1, D], F32, kind="ExternalInput")
    bv_d = nc.dram_tensor("bv", [1, D], F32, kind="ExternalInput")
    cx_d = nc.dram_tensor("clipx", [1, 1], F32, kind="ExternalInput")
    cd_d = nc.dram_tensor("clipd", [1, 1], F32, kind="ExternalInput")
    wg_d = nc.dram_tensor("wg", [DI, D], F32, kind="ExternalInput")
    wu_d = nc.dram_tensor("wu", [DI, D], F32, kind="ExternalInput")
    wd_d = nc.dram_tensor("wd", [D, DI], F32, kind="ExternalInput")
    out_d = nc.dram_tensor("out", [S, D], F32, kind="ExternalOutput")

    cc1i = nc.dram_tensor("cc1i", [128, 1], F32)
    cc1o = nc.dram_tensor("cc1o", [128, 1], F32)
    cc2i = nc.dram_tensor("cc2i", [128, 1], F32)
    cc2o = nc.dram_tensor("cc2o", [128, 1], F32)
    cc1bi = nc.dram_tensor("cc1bi", [128, 1], F32)
    cc1bo = nc.dram_tensor("cc1bo", [128, 1], F32)
    cc2bi = nc.dram_tensor("cc2bi", [128, 1], F32)
    cc2bo = nc.dram_tensor("cc2bo", [128, 1], F32)
    sb1_d = nc.dram_tensor("sbc1", [1, 2], F32)
    sb2_d = nc.dram_tensor("sbc2", [1, 2], F32)

    p = B()

    with ExitStack() as ctx:
        e = ctx.enter_context
        XR = e(nc.sbuf_tensor("XR", [128, NT * D], BF16))
        FM = e(nc.sbuf_tensor("FM", [128, ND * S], BF16))
        AR2 = e(nc.sbuf_tensor("AR2", [128, 49152], BF16))
        SCF = e(nc.sbuf_tensor("SCF", [128, 2816], F32))
        OST = e(nc.sbuf_tensor("OST", [128, 1024], F32))
        PTS = e(nc.sbuf_tensor("PTS", [128, 2048], BF16))
        QTS = e(nc.sbuf_tensor("QTS", [128, 2048], BF16))
        YST = e(nc.sbuf_tensor("YST", [128, 1024], BF16))
        BIA = e(nc.sbuf_tensor("BIA", [1, 3 * D], BF16))
        ONE = e(nc.sbuf_tensor("ONE", [1, 512], BF16))
        IDN = e(nc.sbuf_tensor("IDN", [128, 128], BF16))
        ST = e(nc.sbuf_tensor("ST", [128, 272], F32))
        SCL = e(nc.sbuf_tensor("SCL", [1, 64], F32))
        BNC = e(nc.sbuf_tensor("BNC", [1, 128], F32))
        BNC2 = e(nc.sbuf_tensor("BNC2", [1, 128], F32))
        WDQB = e(nc.sbuf_tensor("WDQB", [128, 2816], BF16))
        YST2 = e(nc.sbuf_tensor("YST2", [128, 1024], BF16))
        WDQC = e(nc.sbuf_tensor("WDQC", [128, 2816], BF16))

        pa = e(nc.psum_tensor("pa", [128, 2048], F32))
        pb = e(nc.psum_tensor("pb", [128, 1024], F32))
        pt = e(nc.psum_tensor("pt", [128, 1024], BF16))
        pu = e(nc.psum_tensor("pu", [128, 512], F32))

        AR2f = AR2.bitcast(F32)
        OSTb = OST.bitcast(BF16)
        YSTf = YST.bitcast(F32)
        YST2f = YST2.bitcast(F32)
        HQS = AR2f[:, 22528:24576]
        FM3 = FM.reshape([128, ND, S])
        pt3 = pt.reshape([128, 8, 128])

        C_RSTD1, C_NEGB1 = 0, 16
        C_RSTD2, C_NEGB2 = 32, 48
        C_COLMAX = 64
        C_UCOL = 80
        C_MV = 104
        C_ST6 = 106
        C_NEGM = 118
        C_RS = 119
        C_WSC = 122
        C_S1B, C_RS1B = 123, 124
        C_SDB, C_RSDB = 125, 126
        C_C32 = 127
        C_WGS = 128
        C_WUS = 150
        C_STD = 172
        C_WDS = 173
        C_M128 = 174
        C_NEGMT = 175
        C_MV2 = 176
        C_WSC2 = 210
        C_ST6B = 216
        C_MVB = 228
        C_STDB = 230
        C_EPS = 231
        C_PMX = 232
        C_RSP = 236
        C_M128B = 244
        C_LNT = 245

        L_CX, L_CD, L_GMAX, L_SIGX, L_S1, L_RS1 = 0, 1, 2, 3, 4, 5
        L_UMAX, L_SIGD, L_SD, L_RSD, L_LMAX = 6, 7, 8, 9, 10
        C_CXB, C_SGXB, C_GXB, C_LXB = 264, 266, 268, 269

        nc.const_aps.register(F32, SCALE_QK, ST[:, C_C32:C_C32 + 1])
        nc.const_aps.register(F32, EPS, ST[:, C_EPS:C_EPS + 1])
        nc.const_aps.register(F32, -0.5, ST[:, C_LNT + 17:C_LNT + 18])

        p.op("pool", lambda g: g.memset(ST[:, C_C32:C_C32 + 1], SCALE_QK),
             writes=[("st32",)])
        p.op("pool", lambda g: g.memset(ST[:, C_EPS:C_EPS + 1], EPS),
             writes=[("steps",)])
        p.op("pool", lambda g: g.memset(ST[:, C_LNT + 17:C_LNT + 18], -0.5),
             writes=[("stmh",)])
        p.op("pool", lambda g: g.memset(ONE[:], 1.0), writes=[("ONE",)])
        p.op("pool", lambda g: g.memset(IDN[:], 0.0), writes=[("IDN",)])
        p.op("pool", lambda g: g.affine_select(
            out=IDN[:], in_=IDN[:], compare_op=ALU.not_equal, fill=1.0,
            base=0, pattern=[[-1, 128]], channel_multiplier=1),
            reads=[("IDN",)], writes=[("IDN",)])

        WQv = AR2[:, 32768:40960].rearrange("p (c n) -> p c n", c=ND)
        WKVv = AR2[:, 40960:49152].rearrange("p (c n) -> p c n", c=ND)
        wq_src = wq_d.ap().rearrange("(c p) n -> p c n", p=128)
        wk_src = wk_d.ap().rearrange("(c p) n -> p c n", p=128)
        wv_src = wv_d.ap().rearrange("(c p) n -> p c n", p=128)
        XRv = XR.rearrange("p (r n) -> p r n", r=NT)
        x_src = x_d.ap().rearrange("(r p) n -> p r n", p=128)

        p.dma("qgp", lambda g: g.dma_start(out=WKVv, in_=wv_src), writes=[("WKV",)])
        p.dma("qgp", lambda g: g.dma_start(
            out=XRv[:, 0:1, :], in_=x_src[:, 0:1, :]), writes=[("XR", 0)])
        p.dma("qgp", lambda g: g.dma_start(
            out=XRv[:, 1:4, :], in_=x_src[:, 1:4, :]),
            writes=[("XR", r) for r in range(1, 4)])
        p.dma("qgp", lambda g: g.dma_start(out=SCL[0:1, L_CX:L_CX + 1], in_=cx_d.ap()),
              writes=[("scl", L_CX)])
        p.dma("qgp", lambda g: g.dma_start(out=SCL[0:1, L_CD:L_CD + 1], in_=cd_d.ap()),
              writes=[("scl", L_CD)])
        p.dma("qgp", lambda g: g.dma_start(out=BIA[0:1, 0:D], in_=bq_d.ap()),
              writes=[("bq",)])
        p.dma("qgp", lambda g: g.dma_start(out=BIA[0:1, D:2 * D], in_=bk_d.ap()),
              writes=[("bk",)])
        p.dma("qgp", lambda g: g.dma_start(out=BIA[0:1, 2 * D:3 * D], in_=bv_d.ap()),
              writes=[("bv",)])
        for i in range(1, 4):
            p.dma("qgp", lambda g, i=i: g.dma_start(
                out=XRv[:, 4 * i:4 * (i + 1), :], in_=x_src[:, 4 * i:4 * (i + 1), :]),
                writes=[("XR", r) for r in range(4 * i, 4 * i + 4)])
        p.dma("qgp", lambda g: g.dma_start(out=WQv, in_=wq_src), writes=[("WQ",)])

        p.dma("qgp", lambda g: g.dma_start(out=sb1_d.ap(), in_=SCL[0:1, 0:2]),
              reads=[("scl", L_CX), ("scl", L_CD)], writes=[("cbd",)])
        p.dma("qgp", lambda g: g.dma_start(out=ST[:, C_CXB:C_CXB + 2], in_=sb1_d.broadcast_to([128, 2])),
              reads=[("cbd",)], writes=[("cxb",)])
        p.op("act", lambda s: s.activation(out=ST[:, C_SGXB:C_SGXB + 2], in_=ST[:, C_CXB:C_CXB + 2], func=AF.Sigmoid),
             reads=[("cxb",)], writes=[("sgxb",)])

        def xr_tile(r):
            return XR[:, r * D:(r + 1) * D]

        def ln1_tile(r):
            par = r % 2
            c6 = C_ST6 if par == 0 else C_ST6B
            cmv = C_MV if par == 0 else C_MVB
            cstd = C_STD if par == 0 else C_STDB
            p.op("dve", lambda v, r=r, c6=c6: v.bn_stats(ST[:, c6:c6 + 6], xr_tile(r)[:, 0:512]),
                 reads=[("XR", r)], writes=[("st6a", par)])
            p.op("dve", lambda v, r=r, c6=c6: v.bn_stats(ST[:, c6 + 6:c6 + 12], xr_tile(r)[:, 512:1024]),
                 reads=[("XR", r)], writes=[("st6b", par)])
            p.op("dve", lambda v, c6=c6, cmv=cmv: v.bn_aggr(ST[:, cmv:cmv + 2], ST[:, c6:c6 + 12]),
                 reads=[("st6a", par), ("st6b", par)], writes=[("mv", par)])
            p.op("act", lambda s, cmv=cmv, cstd=cstd: s.activation(
                out=ST[:, cstd:cstd + 1], in_=ST[:, cmv + 1:cmv + 2], func=AF.Sqrt, bias=EPS),
                reads=[("mv", par), ("steps",)], writes=[("lstd", par)])
            p.op("dve", lambda v, r=r, cstd=cstd: v.reciprocal(ST[:, C_RSTD1 + r:C_RSTD1 + r + 1], ST[:, cstd:cstd + 1]),
                 reads=[("lstd", par)], writes=[("rstd1", r)])
            p.op("dve", lambda v, r=r, cmv=cmv: v.tensor_scalar_mul(
                ST[:, C_NEGB1 + r:C_NEGB1 + r + 1], ST[:, cmv:cmv + 1], -1.0),
                reads=[("mv", par)], writes=[("negb1", r)])
            yb = YST if r % 2 == 0 else YST2
            ky = ("YST", r % 2)
            p.op("dve", lambda v, r=r, yb=yb: v.tensor_scalar(
                yb[:], xr_tile(r), ST[:, C_NEGB1 + r:C_NEGB1 + r + 1],
                ST[:, C_RSTD1 + r:C_RSTD1 + r + 1], ALU.add, ALU.mult),
                reads=[("XR", r), ("rstd1", r), ("negb1", r)], writes=[ky])
            u = p.gdeps([ky, ("IDN",)])
            for c in range(ND):
                p.op("pe", lambda t, c=c, yb=yb: t.transpose(
                    pt[:, c * 128:(c + 1) * 128], yb[:, c * 128:(c + 1) * 128], IDN[:]),
                    writes=[("pt", c)] if c == 0 else (), extra=u if c == 0 else (),
                    noinc=(0 < c < ND - 1),
                    accum=[("pt", cc) for cc in range(1, ND)] if c == ND - 1 else (),
                    reg_reads=[ky, ("IDN",)] if c == ND - 1 else ())
            p.op("dve", lambda v, r=r: v.tensor_copy(
                FM3[:, :, r * 128:(r + 1) * 128], pt3[:, :, :]),
                reads=[("pt", 0), ("pt", ND - 1)], writes=[("FM", r)])

        KTv = AR2[:, 0:16384].rearrange("p (c n) -> p c n", c=ND)
        VTv = AR2[:, 16384:32768].rearrange("p (c n) -> p c n", c=NT)
        gctr = [0]

        def v_group(tc, h):
            bank = ("pb", h)
            keys = [("WKV",), ("bv",), ("ONE",), ("FM", tc)]
            u = p.gdeps(keys)
            p.op("pe", lambda t, tc=tc, h=h: t.matmul(
                pb[:, h * 512:(h + 1) * 512], FM3[:, 0, tc * 128:(tc + 1) * 128],
                WKVv[:, 0, h * 512:(h + 1) * 512], start=True, stop=False),
                writes=[bank], extra=u)
            for dc in range(1, ND):
                p.op("pe", lambda t, tc=tc, h=h, dc=dc: t.matmul(
                    pb[:, h * 512:(h + 1) * 512], FM3[:, dc, tc * 128:(tc + 1) * 128],
                    WKVv[:, dc, h * 512:(h + 1) * 512], start=False, stop=False), noinc=True)
            p.op("pe", lambda t, tc=tc, h=h: t.matmul(
                pb[:, h * 512:(h + 1) * 512], ONE[0:1, 0:128],
                BIA[0:1, 2 * D + h * 512:2 * D + (h + 1) * 512], start=False, stop=True),
                accum=[bank], reg_reads=keys)
            p.op("act", lambda s, tc=tc, h=h: s.activation(
                out=VTv[:, tc, h * 512:(h + 1) * 512], in_=pb[:, h * 512:(h + 1) * 512], func=AF.Copy),
                reads=[bank], writes=[("VT", tc, h)])

        def k_group(c, j):
            b = gctr[0] % 4
            gctr[0] += 1
            bank = ("pa", b)
            keys = [("WKV2",)] + [("FM", rr) for rr in range(4 * j, 4 * j + 4)]
            u = p.gdeps(keys)
            p.op("pe", lambda t, c=c, j=j, b=b: t.matmul(
                pa[:, b * 512:(b + 1) * 512], WKVv[:, 0, c * 128:(c + 1) * 128],
                FM3[:, 0, j * 512:(j + 1) * 512], start=True, stop=False),
                writes=[bank], extra=u)
            for dc in range(1, ND - 1):
                p.op("pe", lambda t, c=c, j=j, dc=dc, b=b: t.matmul(
                    pa[:, b * 512:(b + 1) * 512], WKVv[:, dc, c * 128:(c + 1) * 128],
                    FM3[:, dc, j * 512:(j + 1) * 512], start=False, stop=False), noinc=True)
            p.op("pe", lambda t, c=c, j=j, b=b: t.matmul(
                pa[:, b * 512:(b + 1) * 512], WKVv[:, ND - 1, c * 128:(c + 1) * 128],
                FM3[:, ND - 1, j * 512:(j + 1) * 512], start=False, stop=True),
                accum=[bank], reg_reads=keys)
            p.op("act", lambda s, c=c, j=j, b=b: s.activation(
                out=KTv[:, c, j * 512:(j + 1) * 512], in_=pa[:, b * 512:(b + 1) * 512], func=AF.Copy),
                reads=[bank], writes=[("KT", c, j)])

        for r in range(NT):
            ln1_tile(r)
            v_group(r, 0)
            v_group(r, 1)

        qts_banks = [(pu, 0, ("pu",)), (pb, 0, ("pb", 0)), (pb, 512, ("pb", 1))]

        def qts_buf(c):
            return (QTS, c * 512) if c < 4 else (OSTb, (c - 4) * 512)

        def emit_qts(q):
            for c in range(ND):
                bt, bo, bkey = qts_banks[c % 3]
                keys = [("WQ",), ("bq",), ("ONE",)] + [("FM", 4 * q + z) for z in range(4)]
                u = p.gdeps(keys)
                p.op("pe", lambda t, c=c, q=q, bt=bt, bo=bo: t.matmul(
                    bt[:, bo:bo + 512], WQv[:, 0, c * 128:(c + 1) * 128],
                    FM3[:, 0, q * 512:(q + 1) * 512], start=True, stop=False),
                    writes=[bkey], extra=u)
                for dc in range(1, ND):
                    p.op("pe", lambda t, c=c, q=q, dc=dc, bt=bt, bo=bo: t.matmul(
                        bt[:, bo:bo + 512], WQv[:, dc, c * 128:(c + 1) * 128],
                        FM3[:, dc, q * 512:(q + 1) * 512], start=False, stop=False), noinc=True)
                p.op("pe", lambda t, c=c, bt=bt, bo=bo: t.matmul(
                    bt[:, bo:bo + 512], BIA[0:1, c * 128:(c + 1) * 128],
                    ONE[0:1, 0:512], start=False, stop=True),
                    accum=[bkey], reg_reads=keys)
                qb, qo = qts_buf(c)
                p.op("act", lambda s, c=c, bt=bt, bo=bo, qb=qb, qo=qo: s.activation(
                    out=qb[:, qo:qo + 512], in_=bt[:, bo:bo + 512],
                    func=AF.Copy, scale=SCALE_QK),
                    reads=[bkey, ("st32",)], writes=[("QTS", c)])

        emit_qts(0)

        p.dma("qgp", lambda g: g.dma_start(out=WKVv, in_=wk_src),
              writes=[("WKV",), ("WKV2",)])
        for j in range(4):
            for c in range(ND):
                k_group(c, j)

        Psb0 = AR2[:, 40960:43008]
        Psb1 = AR2[:, 43008:45056]
        fence_psb = p.snap()

        def ln2_stats(i):
            par = i % 2
            c6 = C_ST6 if par == 0 else C_ST6B
            p.op("dve", lambda v, i=i, c6=c6: v.bn_stats(ST[:, c6:c6 + 6], xr_tile(i)[:, 0:512]),
                 reads=[("XR", i)], writes=[("st6a", par)])
            p.op("dve", lambda v, i=i, c6=c6: v.bn_stats(ST[:, c6 + 6:c6 + 12], xr_tile(i)[:, 512:1024]),
                 reads=[("XR", i)], writes=[("st6b", par)])
            p.op("dve", lambda v, i=i, c6=c6: v.bn_aggr(ST[:, C_MV2 + 2 * i:C_MV2 + 2 * i + 2], ST[:, c6:c6 + 12]),
                 reads=[("st6a", par), ("st6b", par)], writes=[("mv2", i)])
            hs = HQS[:, (i % 2) * 1024:(i % 2) * 1024 + 1024]
            p.op("dve", lambda v, i=i: v.tensor_scalar_mul(
                ST[:, C_NEGMT:C_NEGMT + 1], ST[:, C_MV2 + 2 * i:C_MV2 + 2 * i + 1], -1.0),
                reads=[("mv2", i)], writes=[("negmt",)])
            p.op("act", lambda s, i=i, hs=hs: s.activation(
                out=hs, in_=xr_tile(i), func=AF.Abs, bias=ST[:, C_NEGMT:C_NEGMT + 1]),
                reads=[("XR", i), ("negmt",)], writes=[("hqs", i % 2)])
            p.op("dve", lambda v, i=i, hs=hs: v.reduce_max(
                ST[:, C_COLMAX + i:C_COLMAX + i + 1], hs, axis=AX.X),
                reads=[("hqs", i % 2)], writes=[("colmax", i)])

        def psb(i):
            return Psb0 if i % 2 == 0 else Psb1

        def emit_scores(i):
            sub = i % 4
            def qsl(c):
                qb, qo = qts_buf(c)
                return qb[:, qo + sub * 128:qo + sub * 128 + 128]
            for j in range(4):
                bank = ("pa", j)
                keys = [("QTS", c) for c in range(ND)] + [("KT", c, j) for c in range(ND)]
                u = p.gdeps(keys)
                p.op("pe", lambda t, j=j, a=qsl(0): t.matmul(
                    pa[:, j * 512:(j + 1) * 512], a,
                    KTv[:, 0, j * 512:(j + 1) * 512], start=True, stop=False),
                    writes=[bank], extra=u)
                for c in range(1, ND - 1):
                    p.op("pe", lambda t, j=j, a=qsl(c), c=c: t.matmul(
                        pa[:, j * 512:(j + 1) * 512], a,
                        KTv[:, c, j * 512:(j + 1) * 512], start=False, stop=False), noinc=True)
                p.op("pe", lambda t, j=j, a=qsl(ND - 1): t.matmul(
                    pa[:, j * 512:(j + 1) * 512], a,
                    KTv[:, ND - 1, j * 512:(j + 1) * 512], start=False, stop=True),
                    accum=[bank], reg_reads=keys)
                p.op("dve", lambda v, j=j: v.reduce_max(
                    ST[:, C_PMX + j:C_PMX + j + 1], pa[:, j * 512:(j + 1) * 512], axis=AX.X),
                    reads=[bank], writes=[("pmx", j)])

        def emit_max(i):
            p.op("dve", lambda v: v.reduce_max(ST[:, C_NEGM:C_NEGM + 1],
                                               ST[:, C_PMX:C_PMX + 4], axis=AX.X, negate=True),
                 reads=[("pmx", j) for j in range(4)], writes=[("negm",)])

        def emit_softmax(i):
            par = i % 2
            for j in range(4):
                p.op("act", lambda s, i=i, j=j, par=par: s.activation(
                    out=psb(i)[:, j * 512:(j + 1) * 512], in_=pa[:, j * 512:(j + 1) * 512],
                    func=AF.Exp, bias=ST[:, C_NEGM:C_NEGM + 1],
                    accum_out=ST[:, C_RSP + 4 * par + j:C_RSP + 4 * par + j + 1]),
                    reads=[("pa", j), ("negm",)],
                    writes=[("Psb4", par, j), ("sumexp", par, j)], extra=fence_psb)
            p.op("dve", lambda v, par=par: v.reduce_sum(
                ST[:, C_RS + par:C_RS + par + 1], ST[:, C_RSP + 4 * par:C_RSP + 4 * par + 4], axis=AX.X),
                reads=[("sumexp", par, j) for j in range(4)], writes=[("sumt", par)])
            p.op("dve", lambda v, par=par: v.reciprocal(
                ST[:, C_RS + par:C_RS + par + 1], ST[:, C_RS + par:C_RS + par + 1]),
                reads=[("sumt", par)], writes=[("rs", par)])

        def emit_pv_head(i):
            for half in range(2):
                u = p.gdeps([("Psb4", i % 2, j) for j in range(4)] + [("IDN",)])
                for c in range(8):
                    tc = half * 8 + c
                    p.op("pe", lambda t, tc=tc, c=c, i=i: t.transpose(
                        pt[:, c * 128:(c + 1) * 128], psb(i)[:, tc * 128:(tc + 1) * 128], IDN[:]),
                        writes=[("pt", c)] if c == 0 else (), extra=u if c == 0 else (),
                        noinc=(0 < c < 7),
                        accum=[("pt", cc) for cc in range(1, 8)] if c == 7 else (),
                        reg_reads=[("Psb4", i % 2, j) for j in range(4)] + [("IDN",)] if c == 7 else ())
                p.op("act", lambda s, half=half: s.activation(
                    out=PTS[:, half * 1024:(half + 1) * 1024], in_=pt[:], func=AF.Copy),
                    reads=[("pt", 0), ("pt", 7)], writes=[("PTs", half)])
            for h in range(2):
                bank = ("pb", h)
                keys = [("PTs", 0), ("PTs", 1)] + [("VT", tc, h) for tc in range(NT)]
                u = p.gdeps(keys)
                p.op("pe", lambda t, h=h: t.matmul(
                    pb[:, h * 512:(h + 1) * 512], PTS[:, 0:128],
                    VTv[:, 0, h * 512:(h + 1) * 512], start=True, stop=False),
                    writes=[bank], extra=u)
                for tc in range(1, NT - 1):
                    p.op("pe", lambda t, h=h, tc=tc: t.matmul(
                        pb[:, h * 512:(h + 1) * 512], PTS[:, tc * 128:(tc + 1) * 128],
                        VTv[:, tc, h * 512:(h + 1) * 512], start=False, stop=False), noinc=True)
                p.op("pe", lambda t, h=h: t.matmul(
                    pb[:, h * 512:(h + 1) * 512], PTS[:, (NT - 1) * 128:NT * 128],
                    VTv[:, NT - 1, h * 512:(h + 1) * 512], start=False, stop=True),
                    accum=[bank], reg_reads=keys)
        def emit_pv_tail(i):
            p.op("dve", lambda v, i=i: v.scalar_tensor_tensor(
                xr_tile(i), pb[:], ST[:, C_RS + i % 2:C_RS + i % 2 + 1], xr_tile(i),
                ALU.mult, ALU.add),
                reads=[("pb", 0), ("pb", 1), ("rs", i % 2), ("XR", i)], writes=[("XR", i)])
            ln2_stats(i)

        def global_max_launch(cci, cco, tag, src_col, bdst, bkeys, defer_read=False):
            p.dma("qgp", lambda g, cci=cci, src_col=src_col: g.dma_start(
                out=cci.ap(), in_=ST[:, src_col:src_col + 1]),
                reads=[("m128", tag)], writes=[("cci", tag)])
            p.op("pool", lambda g, cci=cci, cco=cco: g.collective_compute(
                "AllReduce", ALU.max, replica_groups=[list(range(8))],
                ins=[cci.ap().opt()], outs=[cco.ap().opt()]),
                reads=[("cci", tag)], writes=[("cco", tag)])

            def emit_read():
                p.dma("qgp", lambda g, cco=cco, bdst=bdst: g.dma_start(
                    out=bdst, in_=cco.ap().rearrange("p one -> one p").broadcast_to([128, 128])),
                    reads=[("cco", tag)], writes=list(bkeys))
            if defer_read:
                return emit_read
            emit_read()

        def global_max_finish2(l_cell, bkey_a, bkey_b):
            p.op("dve", lambda v, l_cell=l_cell: v.reduce_max(
                SCL[0:1, l_cell:l_cell + 1], BNC[0:1, :], axis=AX.X),
                reads=[bkey_a], writes=[("gma", l_cell)])
            p.op("dve", lambda v, l_cell=l_cell: v.reduce_max(
                SCL[0:1, L_LMAX:L_LMAX + 1], BNC2[0:1, :], axis=AX.X),
                reads=[bkey_b], writes=[("gmb", l_cell)])
            p.op("dve", lambda v, l_cell=l_cell: v.tensor_tensor(
                SCL[0:1, l_cell:l_cell + 1], SCL[0:1, l_cell:l_cell + 1],
                SCL[0:1, L_LMAX:L_LMAX + 1], ALU.max),
                reads=[("gma", l_cell), ("gmb", l_cell)], writes=[("scl", l_cell)])

        def rstd2_batch(lo, hi, tag):
            n = hi - lo
            vv_ = ST[:, C_MV2 + 2 * lo:C_MV2 + 2 * hi].rearrange(
                "p (r two) -> p r two", two=2)[:, :, 1]
            mm_ = ST[:, C_MV2 + 2 * lo:C_MV2 + 2 * hi].rearrange(
                "p (r two) -> p r two", two=2)[:, :, 0]
            mv2k = [("mv2", i) for i in range(lo, hi)]
            p.op("act", lambda s, vv_=vv_, lo=lo, n=n: s.activation(
                out=ST[:, C_LNT + lo:C_LNT + lo + n], in_=vv_, func=AF.Ln, bias=EPS),
                reads=mv2k + [("steps",)], writes=[("lnt", tag)])
            p.op("act", lambda s, lo=lo, n=n: s.activation(
                out=ST[:, C_RSTD2 + lo:C_RSTD2 + lo + n],
                in_=ST[:, C_LNT + lo:C_LNT + lo + n], func=AF.Exp, scale=-0.5),
                reads=[("lnt", tag), ("stmh",)], writes=[("rstd2", tag)])
            p.op("dve", lambda v, mm_=mm_, lo=lo, n=n: v.tensor_scalar_mul(
                ST[:, C_NEGB2 + lo:C_NEGB2 + lo + n], mm_, -1.0),
                reads=mv2k, writes=[("negb2", tag)])
            p.op("dve", lambda v, lo=lo, n=n: v.tensor_tensor(
                ST[:, C_COLMAX + lo:C_COLMAX + lo + n], ST[:, C_COLMAX + lo:C_COLMAX + lo + n],
                ST[:, C_RSTD2 + lo:C_RSTD2 + lo + n], ALU.mult),
                reads=[("colmax", i) for i in range(lo, hi)] + [("rstd2", tag)],
                writes=[("colmaxn", tag)])

        def emit_gmaxA():
            rstd2_batch(0, 13, "a")
            p.op("dve", lambda v: v.reduce_max(ST[:, C_M128:C_M128 + 1],
                                               ST[:, C_COLMAX:C_COLMAX + 13], axis=AX.X),
                 reads=[("colmaxn", "a")], writes=[("m128", "c1a")])
            global_max_launch(cc1i, cc1o, "c1a", C_M128, YSTf[:, 0:128],
                              [("bnca",), ("YST", 0)])

        for i in range(NT):
            if i % 4 == 0 and i > 0:
                emit_qts(i // 4)
            emit_scores(i)
            emit_max(i)
            if i > 0:
                emit_pv_head(i - 1)
            emit_softmax(i)
            if i > 0:
                emit_pv_tail(i - 1)
            if i == NT - 2:
                emit_gmaxA()
        emit_pv_head(NT - 1)
        emit_pv_tail(NT - 1)

        SG = AR2

        WINT4 = [WDQB[:, 0:1024], WDQB[:, 1024:2048],
                 QTS[:, 0:1024], QTS[:, 1024:2048]]
        WTB4 = [OSTb[:, 0:1024], OSTb[:, 1024:2048],
                PTS[:, 0:1024], PTS[:, 1024:2048]]

        STAGE4 = [SCF[:, 0:1024], SCF[:, 1024:2048],
                  HQS[:, 0:1024], HQS[:, 1024:2048]]

        def prep_quant(w_src_d, c_ws, phase, c):
                b4 = c % 4
                stage = STAGE4[b4]
                wint = WINT4[b4]
                kst = ("wstage", b4)
                kwi = ("wint", b4)
                src = w_src_d.ap()[c * 128:(c + 1) * 128, :]
                ex = FXQ if (phase == "g" and 2 <= c < 4) else ()
                p.dma("qsp", lambda sy, src=src, stage=stage: sy.dma_start(out=stage, in_=src),
                      writes=[kst], extra=ex)
                p.op("dve", lambda v, c=c, stage=stage, c_ws=c_ws: v.reduce_max(
                    ST[:, c_ws + c:c_ws + c + 1], stage, axis=AX.X, apply_absolute_value=True),
                    reads=[kst], writes=[("ws", phase, c)])
                p.op("dve", lambda v, c=c, c_ws=c_ws: v.tensor_scalar_mul(
                    ST[:, c_ws + c:c_ws + c + 1], ST[:, c_ws + c:c_ws + c + 1], INV127),
                    reads=[("ws", phase, c)], writes=[("ws", phase, c)])
                p.op("dve", lambda v, c=c, c_ws=c_ws: v.reciprocal(
                    ST[:, C_WSC:C_WSC + 1], ST[:, c_ws + c:c_ws + c + 1]),
                    reads=[("ws", phase, c)], writes=[("rws",)])
                p.op("dve", lambda v, stage=stage: v.tensor_scalar(
                    stage, stage, ST[:, C_WSC:C_WSC + 1], None, ALU.mult),
                    reads=[kst, ("rws",)], writes=[kst])
                p.op("dve", lambda v, stage=stage, wint=wint: v.tensor_scalar(
                    wint, stage, MAGIC, MAGIC, ALU.add, ALU.subtract),
                    reads=[kst], writes=[kwi],
                    extra=FXQ if (phase == "g" and 2 <= c < 4) else ())

        def prep_transp(phase, c):
                b4 = c % 4
                wint = WINT4[b4]
                wtb = WTB4[b4]
                kwi = ("wint", b4)
                kwt = ("wt", b4)
                u = p.gdeps([kwi, ("IDN",)])
                for h in range(ND):
                    p.op("pe", lambda t, h=h, wint=wint: t.transpose(
                        pt[:, h * 128:(h + 1) * 128], wint[:, h * 128:(h + 1) * 128], IDN[:]),
                        writes=[("pt", h)] if h == 0 else (), extra=u if h == 0 else (),
                        noinc=(0 < h < ND - 1),
                        accum=[("pt", hh) for hh in range(1, ND)] if h == ND - 1 else (),
                        reg_reads=[kwi, ("IDN",)] if h == ND - 1 else ())
                p.op("act", lambda s, wtb=wtb: s.activation(out=wtb, in_=pt[:], func=AF.Copy),
                     reads=[("pt", 0), ("pt", ND - 1)], writes=[kwt],
                     extra=FXQ if (phase == "g" and 2 <= c < 4) else (FA if (phase == "g" and c < 2) else ()))
        BANK6 = [("pa", 0), ("pa", 1), ("pa", 2), ("pa", 3), ("pb", 0), ("pb", 1)]

        def bank_ap(bk):
            return pa[:, bk[1] * 512:(bk[1] + 1) * 512] if bk[0] == "pa" else \
                   pb[:, bk[1] * 512:(bk[1] + 1) * 512]

        def ffn_mm(c_ws, phase, c):
                b4 = c % 4
                wtb = WTB4[b4]
                kwt = ("wt", b4)
                p.op("dve", lambda v, c=c, c_ws=c_ws: v.tensor_scalar(
                    ST[:, c_ws + c:c_ws + c + 1], ST[:, c_ws + c:c_ws + c + 1],
                    ST[:, C_S1B:C_S1B + 1], None, ALU.mult),
                    reads=[("ws", phase, c), ("s1b",)], writes=[("sv", phase, c)])
                for j in range(4):
                    bank = BANK6[gctr[0] % 6]
                    gctr[0] += 1
                    bap = bank_ap(bank)
                    keys = [kwt] + [("FM", rr) for rr in range(4 * j, 4 * j + 4)]
                    u = p.gdeps(keys)
                    p.op("pe", lambda t, j=j, wtb=wtb, bap=bap: t.matmul(
                        bap, wtb[:, 0:128],
                        FM3[:, 0, j * 512:(j + 1) * 512], start=True, stop=False),
                        writes=[bank], extra=u)
                    for h in range(1, ND - 1):
                        p.op("pe", lambda t, j=j, h=h, wtb=wtb, bap=bap: t.matmul(
                            bap, wtb[:, h * 128:(h + 1) * 128],
                            FM3[:, h, j * 512:(j + 1) * 512], start=False, stop=False), noinc=True)
                    p.op("pe", lambda t, j=j, wtb=wtb, bap=bap: t.matmul(
                        bap, wtb[:, (ND - 1) * 128:ND * 128],
                        FM3[:, ND - 1, j * 512:(j + 1) * 512], start=False, stop=True),
                        accum=[bank], reg_reads=keys)
                    sgt = SG[:, c * 2048 + j * 512:c * 2048 + (j + 1) * 512]
                    if phase == "g":
                        p.op("act", lambda s, c=c, sgt=sgt, c_ws=c_ws, bap=bap: s.activation(
                            out=sgt, in_=bap, func=AF.Silu,
                            scale=ST[:, c_ws + c:c_ws + c + 1]),
                            reads=[bank, ("sv", phase, c)], writes=[("SG", c, j)],
                            extra=FXQ)
                    else:
                        ups = YST[:, (j % 2) * 512:(j % 2) * 512 + 512]
                        p.op("act", lambda s, c=c, ups=ups, c_ws=c_ws, bap=bap: s.activation(
                            out=ups, in_=bap, func=AF.Copy,
                            scale=ST[:, c_ws + c:c_ws + c + 1]),
                            reads=[bank, ("sv", phase, c)], writes=[("ups", j % 2)])
                        p.op("dve", lambda v, sgt=sgt, ups=ups, c=c, j=j: v.tensor_tensor(
                            sgt, sgt, ups, ALU.mult),
                            reads=[("ups", j % 2), ("SG", c, j)], writes=[("SG", c, j)])
                if phase == "u":
                    p.op("dve", lambda v, c=c: v.reduce_max(
                        ST[:, C_UCOL + c:C_UCOL + c + 1], SG[:, c * 2048:(c + 1) * 2048],
                        axis=AX.X, apply_absolute_value=True),
                        reads=[("SG", c, j) for j in range(4)], writes=[("ucol", c)])
                    if c == 15:
                        p.op("dve", lambda v: v.reduce_max(
                            ST[:, C_M128:C_M128 + 1], ST[:, C_UCOL:C_UCOL + 16], axis=AX.X),
                            reads=[("ucol", cc) for cc in range(16)], writes=[("m128", "c2a")])
                        global_max_launch(cc2i, cc2o, "c2a", C_M128, YST2f[:, 0:128],
                                          [("bnca2",), ("YST", 1)])

        def ffn_proj(w_src_d, c_ws, phase, skip_pre=0, tail_hook=None):
            for c in range(skip_pre, 3):
                prep_quant(w_src_d, c_ws, phase, c)
            for c in range(skip_pre, 2):
                prep_transp(phase, c)
            for c in range(NI):
                if c + 3 < NI:
                    prep_quant(w_src_d, c_ws, phase, c + 3)
                ffn_mm(c_ws, phase, c)
                if c + 2 < NI:
                    prep_transp(phase, c + 2)
                if tail_hook is not None:
                    tail_hook(c)


        FA = p.snap()

        rstd2_batch(13, 16, "b")
        p.op("dve", lambda v: v.reduce_max(ST[:, C_M128B:C_M128B + 1],
                                           ST[:, C_COLMAX + 13:C_COLMAX + 16], axis=AX.X),
             reads=[("colmaxn", "b")], writes=[("m128", "c1b")])
        global_max_launch(cc1bi, cc1bo, "c1b", C_M128B, YST2f[:, 0:128],
                          [("bncb",), ("YST", 1)])
        prep_quant(wg_d, C_WGS, "g", 0)
        prep_quant(wg_d, C_WGS, "g", 1)
        prep_transp("g", 0)
        prep_transp("g", 1)

        p.op("dve", lambda v: v.reduce_max(ST[:, C_GXB:C_GXB + 1], YSTf[:, 0:128], axis=AX.X),
             reads=[("bnca",), ("YST", 0)], writes=[("gmxa",)])
        p.op("dve", lambda v: v.reduce_max(ST[:, C_LXB:C_LXB + 1], YST2f[:, 0:128], axis=AX.X),
             reads=[("bncb",), ("YST", 1)], writes=[("gmxb",)])
        p.op("dve", lambda v: v.tensor_tensor(ST[:, C_GXB:C_GXB + 1], ST[:, C_GXB:C_GXB + 1],
                                              ST[:, C_LXB:C_LXB + 1], ALU.max),
             reads=[("gmxa",), ("gmxb",)], writes=[("gmx",)])
        p.op("dve", lambda v: v.tensor_tensor(ST[:, C_S1B:C_S1B + 1], ST[:, C_GXB:C_GXB + 1],
                                              ST[:, C_SGXB:C_SGXB + 1], ALU.mult),
             reads=[("gmx",), ("sgxb",)], writes=[("s1b",)])
        p.op("dve", lambda v: v.tensor_scalar_mul(ST[:, C_S1B:C_S1B + 1], ST[:, C_S1B:C_S1B + 1], INV127),
             reads=[("s1b",)], writes=[("s1b",)])
        p.op("dve", lambda v: v.reciprocal(ST[:, C_RS1B:C_RS1B + 1], ST[:, C_S1B:C_S1B + 1]),
             reads=[("s1b",)], writes=[("rs1b",)])
        p.op("dve", lambda v: v.tensor_scalar(
            ST[:, C_RSTD2:C_RSTD2 + 16], ST[:, C_RSTD2:C_RSTD2 + 16],
            ST[:, C_RS1B:C_RS1B + 1], None, ALU.mult),
            reads=[("rstd2", "a"), ("rstd2", "b"), ("rs1b",)], writes=[("rstd2q",)])

        for r in range(NT):
            hq = ("hqs", r % 2)
            hs = HQS[:, (r % 2) * 1024:(r % 2) * 1024 + 1024]
            p.op("dve", lambda v, r=r, hs=hs: v.tensor_scalar(
                hs, xr_tile(r), ST[:, C_NEGB2 + r:C_NEGB2 + r + 1],
                ST[:, C_RSTD2 + r:C_RSTD2 + r + 1], ALU.add, ALU.mult),
                reads=[("XR", r), ("negb2", "a"), ("negb2", "b"), ("rstd2q",)], writes=[hq])
            p.op("dve", lambda v, hs=hs: v.tensor_scalar(
                hs, hs, MAGIC, MAGIC, ALU.add, ALU.subtract),
                reads=[hq], writes=[hq])
            yb = YST if r % 2 == 0 else YST2
            ky = ("YST", r % 2)
            p.op("dve", lambda v, hs=hs, yb=yb: v.tensor_scalar(
                yb[:], hs, QMAX, -128.0, ALU.min, ALU.max),
                reads=[hq], writes=[ky])
            u = p.gdeps([ky, ("IDN",)])
            for c in range(ND):
                p.op("pe", lambda t, c=c, yb=yb: t.transpose(
                    pt[:, c * 128:(c + 1) * 128], yb[:, c * 128:(c + 1) * 128], IDN[:]),
                    writes=[("pt", c)] if c == 0 else (), extra=u if c == 0 else (),
                    noinc=(0 < c < ND - 1),
                    accum=[("pt", cc) for cc in range(1, ND)] if c == ND - 1 else (),
                    reg_reads=[ky, ("IDN",)] if c == ND - 1 else ())
            p.op("act", lambda s, r=r: s.activation(
                out=FM3[:, :, r * 128:(r + 1) * 128], in_=pt3[:, :, :], func=AF.Copy),
                reads=[("pt", 0), ("pt", ND - 1)], writes=[("FM", r)], extra=FA)

        FXQ = p.snap()

        WDT = FM
        WDTv = FM[:, 0:11264].rearrange("p (c n) -> p c n", c=NI)
        WDQ_A = FM[:, 11264:14080]
        WDQ_B = WDQB
        WDQ_C = WDQC

        def wd_quant(c, WDQ, magic_extra=(), magic_writes=()):
            stage = SCF[:, 0:2816]
            src = wd_d.ap()[c * 128:(c + 1) * 128, :]
            kq = ("wdq", c % 3)
            dw = [("wdstage",)] + ([("wstage", 0), ("wstage", 1)] if c == 0 else [])
            p.dma("qgp", lambda g, src=src, stage=stage: g.dma_start(out=stage, in_=src),
                  writes=dw)
            p.op("dve", lambda v, stage=stage: v.reduce_max(
                ST[:, C_WDS:C_WDS + 1], stage, axis=AX.X, apply_absolute_value=True),
                reads=[("wdstage",)], writes=[("wds",)])
            p.op("dve", lambda v: v.tensor_scalar_mul(ST[:, C_WDS:C_WDS + 1], ST[:, C_WDS:C_WDS + 1], INV127),
                 reads=[("wds",)], writes=[("wds",)])
            p.op("dve", lambda v: v.reciprocal(ST[:, C_WSC2:C_WSC2 + 1], ST[:, C_WDS:C_WDS + 1]),
                 reads=[("wds",)], writes=[("rws2",)])
            p.op("act", lambda s, stage=stage: s.activation(
                out=stage, in_=stage, func=AF.Copy, scale=ST[:, C_WSC2:C_WSC2 + 1]),
                reads=[("wdstage",), ("rws2",)], writes=[("wdstage",)])
            p.op("dve", lambda v, stage=stage, WDQ=WDQ: v.tensor_scalar(
                WDQ[:], stage, MAGIC, MAGIC, ALU.add, ALU.subtract),
                reads=[("wdstage",)], writes=[kq] + list(magic_writes), extra=magic_extra)
            p.op("act", lambda s, WDQ=WDQ: s.activation(
                out=WDQ[:], in_=WDQ[:], func=AF.Copy, scale=ST[:, C_WDS:C_WDS + 1]),
                reads=[kq, ("wds",)], writes=[kq])

        def wd_transp(c, WDQ, fence=()):
            kq = ("wdq", c % 3)
            sub = WDTv[:, :, (c % 4) * 128:(c % 4) * 128 + 128]
            for i0 in range(0, NI, 8):
                n = min(8, NI - i0)
                u = p.gdeps([kq, ("IDN",)]) + (list(fence) if i0 == 0 else [])
                for i in range(n):
                    p.op("pe", lambda t, i=i, i0=i0, WDQ=WDQ: t.transpose(
                        pt[:, i * 128:(i + 1) * 128], WDQ[:, (i0 + i) * 128:(i0 + i + 1) * 128], IDN[:]),
                        writes=[("pt", i)] if i == 0 else (), extra=u if i == 0 else (),
                        noinc=(0 < i < n - 1),
                        accum=[("pt", ii) for ii in range(1, n)] if i == n - 1 else (),
                        reg_reads=[kq, ("IDN",)] if i == n - 1 else ())
                p.op("dve", lambda v, i0=i0, n=n, sub=sub: v.tensor_copy(
                    sub[:, i0:i0 + n, :], pt3[:, 0:n, :]),
                    reads=[("pt", 0), ("pt", n - 1)], writes=[("WDT", c % 4)])

        def wdq_of(c):
            return [WDQ_B, WDQ_C, WDQ_A][c % 3]

        def wd_tail_hook(c):
            if c == 19:
                wd_quant(0, wdq_of(0), magic_writes=[("wint", 0), ("wint", 1)])
            elif c == 20:
                wd_quant(1, wdq_of(1))

        ffn_proj(wg_d, C_WGS, "g", skip_pre=2)
        ffn_proj(wu_d, C_WUS, "u", tail_hook=wd_tail_hook)

        FW = p.snap()
        FWPE = [("pe", p.cnt["pe"])]

        p.op("dve", lambda v: v.reduce_max(ST[:, C_M128B:C_M128B + 1],
                                           ST[:, C_UCOL + 16:C_UCOL + NI], axis=AX.X),
             reads=[("ucol", c) for c in range(16, NI)], writes=[("m128", "c2b")])
        read2b = global_max_launch(cc2bi, cc2bo, "c2b", C_M128B, YSTf[:, 0:128],
                                   [("bncb2",), ("ups", 0)], defer_read=True)
        wd_quant(2, wdq_of(2), magic_extra=FWPE)
        wd_transp(0, wdq_of(0), fence=FWPE)
        wd_transp(1, wdq_of(1), fence=FWPE)
        wd_transp(2, wdq_of(2), fence=FWPE)
        wd_quant(3, wdq_of(3))
        wd_transp(3, wdq_of(3), fence=FWPE)
        read2b()

        p.op("dve", lambda v: v.reduce_max(ST[:, C_GXB:C_GXB + 1], YST2f[:, 0:128], axis=AX.X),
             reads=[("bnca2",), ("YST", 1)], writes=[("gmxa",), ("gmx",)])
        p.op("dve", lambda v: v.reduce_max(ST[:, C_LXB:C_LXB + 1], YSTf[:, 0:128], axis=AX.X),
             reads=[("bncb2",), ("ups", 0)], writes=[("gmxb",)])
        p.op("dve", lambda v: v.tensor_tensor(ST[:, C_GXB:C_GXB + 1], ST[:, C_GXB:C_GXB + 1],
                                              ST[:, C_LXB:C_LXB + 1], ALU.max),
             reads=[("gmxa",), ("gmxb",)], writes=[("gmx",)])
        p.op("dve", lambda v: v.tensor_tensor(ST[:, C_SDB:C_SDB + 1], ST[:, C_GXB:C_GXB + 1],
                                              ST[:, C_SGXB + 1:C_SGXB + 2], ALU.mult),
             reads=[("gmx",), ("sgxb",)], writes=[("sdb",)])
        p.op("dve", lambda v: v.tensor_scalar_mul(ST[:, C_SDB:C_SDB + 1], ST[:, C_SDB:C_SDB + 1], INV127),
             reads=[("sdb",)], writes=[("sdb",)])
        p.op("dve", lambda v: v.reciprocal(ST[:, C_RSDB:C_RSDB + 1], ST[:, C_SDB:C_SDB + 1]),
             reads=[("sdb",)], writes=[("rsdb",)])

        hu = 0
        for j in range(2):
            for c in range(NI):
                par = hu % 2
                ex = FW if hu < 2 else ()
                hu += 1
                hq = ("hqs", par)
                hs = HQS[:, par * 1024:par * 1024 + 1024]
                sgs = SG[:, c * 2048 + j * 1024:c * 2048 + (j + 1) * 1024]
                rk = [("SG", c, 2 * j), ("SG", c, 2 * j + 1)]
                p.op("act", lambda s, hs=hs, sgs=sgs: s.activation(
                    out=hs, in_=sgs, func=AF.Copy, scale=ST[:, C_RSDB:C_RSDB + 1]),
                    reads=rk + [("rsdb",)], writes=[hq], extra=ex)
                p.op("dve", lambda v, hs=hs: v.tensor_scalar(
                    hs, hs, MAGIC, MAGIC, ALU.add, ALU.subtract),
                    reads=[hq], writes=[hq])
                p.op("dve", lambda v, hs=hs, sgs=sgs, rk=tuple(rk): v.tensor_scalar(
                    sgs, hs, QMAX, -128.0, ALU.min, ALU.max),
                    reads=[hq] + list(rk), writes=list(rk))

        out_v = out_d.ap().rearrange("(r p) n -> p r n", p=128)
        wdt_keys = [("WDT", q) for q in range(4)]

        def down_half(H, hook=None):
            for tb in range(NT):
                bank = BANK6[gctr[0] % 6]
                gctr[0] += 1
                bap = bank_ap(bank)
                keys = wdt_keys + [("SG", c, tb // 4) for c in range(NI)]
                trickle = (H == 0 and tb == 0)
                u = p.gdeps(wdt_keys + [("SG", 0, 0)]) if trickle else p.gdeps(keys)
                p.op("pe", lambda t, tb=tb, bap=bap: t.matmul(
                    bap,
                    SG[:, tb * 128:tb * 128 + 128],
                    WDTv[:, 0, :], start=True, stop=False),
                    writes=[bank], extra=u)
                for c in range(1, NI - 1):
                    p.op("pe", lambda t, tb=tb, c=c, bap=bap: t.matmul(
                        bap,
                        SG[:, c * 2048 + tb * 128:c * 2048 + tb * 128 + 128],
                        WDTv[:, c, :], start=False, stop=False),
                        noinc=not trickle,
                        reads=[("SG", c, 0)] if trickle else (),
                        accum=[bank] if trickle else ())
                p.op("pe", lambda t, tb=tb, bap=bap: t.matmul(
                    bap,
                    SG[:, (NI - 1) * 2048 + tb * 128:(NI - 1) * 2048 + tb * 128 + 128],
                    WDTv[:, NI - 1, :], start=False, stop=True),
                    reads=[("SG", NI - 1, 0)] if trickle else (),
                    accum=[bank], reg_reads=keys)
                xs = XR[:, tb * D + H * 512:tb * D + (H + 1) * 512]
                par = (tb + H) % 2
                ot = OST[:, par * 512:(par + 1) * 512]
                p.op("dve", lambda v, tb=tb, xs=xs, ot=ot, bap=bap: v.scalar_tensor_tensor(
                    ot, bap, ST[:, C_SDB:C_SDB + 1],
                    xs, ALU.mult, ALU.add),
                    reads=[bank, ("sdb",), ("XR", tb)], writes=[("XO", par)])
                p.dma("qsp", lambda sy, tb=tb, H=H, ot=ot: sy.dma_start(
                    out=out_v[:, tb, H * 512:(H + 1) * 512], in_=ot),
                    reads=[("XO", par)])
                if hook is not None:
                    hook(tb)

        def wd46_hook(tb):
            if tb == 0:
                wd_quant(4, wdq_of(4))
            elif tb == 3:
                wd_quant(5, wdq_of(5))
            elif tb == 7:
                wd_quant(6, wdq_of(6))

        down_half(0, hook=wd46_hook)
        wd_transp(4, wdq_of(4))
        wd_quant(7, wdq_of(7))
        wd_transp(5, wdq_of(5))
        wd_transp(6, wdq_of(6))
        wd_transp(7, wdq_of(7))
        down_half(1)

        p.run(nc)
    return nc


_NC_CACHE = {}


def kernel(L0_attn_X, L0_attn_W_q, L0_attn_W_k, L0_attn_W_v,
           L0_attn_b_q, L0_attn_b_k, L0_attn_b_v,
           L0_ffn_clip_x, L0_ffn_clip_down,
           L0_ffn_w_gate, L0_ffn_w_up, L0_ffn_w_down):
    if "nc" not in _NC_CACHE:
        _NC_CACHE["nc"] = build_nc()
    nc = _NC_CACHE["nc"]

    X = np.ascontiguousarray(np.asarray(L0_attn_X, dtype=np.float32))
    shared = {
        "wq": np.ascontiguousarray(np.asarray(L0_attn_W_q, dtype=np.float32)),
        "wk": np.ascontiguousarray(np.asarray(L0_attn_W_k, dtype=np.float32)),
        "wv": np.ascontiguousarray(np.asarray(L0_attn_W_v, dtype=np.float32)),
        "bq": np.asarray(L0_attn_b_q, dtype=np.float32).reshape(1, D),
        "bk": np.asarray(L0_attn_b_k, dtype=np.float32).reshape(1, D),
        "bv": np.asarray(L0_attn_b_v, dtype=np.float32).reshape(1, D),
        "clipx": np.asarray(L0_ffn_clip_x, dtype=np.float32).reshape(1, 1),
        "clipd": np.asarray(L0_ffn_clip_down, dtype=np.float32).reshape(1, 1),
        "wg": np.ascontiguousarray(np.asarray(L0_ffn_w_gate, dtype=np.float32)),
        "wu": np.ascontiguousarray(np.asarray(L0_ffn_w_up, dtype=np.float32)),
        "wd": np.ascontiguousarray(np.asarray(L0_ffn_w_down, dtype=np.float32)),
    }
    in_maps = [dict(shared, x=np.ascontiguousarray(X[c])) for c in range(8)]
    res = run_bass_kernel_spmd(nc, in_maps, core_ids=list(range(8)))
    out = np.stack([np.asarray(res.results[c]["out"]) for c in range(8)], axis=0)
    return out.astype(np.float32)


# version 26
# speedup vs baseline: 1.0122x; 1.0122x over previous
import numpy as np
import concourse.bass as bass
import concourse.mybir as mybir
from concourse.bass_utils import run_bass_kernel_spmd
from contextlib import ExitStack

F32 = mybir.dt.float32
BF16 = mybir.dt.bfloat16
AF = mybir.ActivationFunctionType
ALU = mybir.AluOpType
AX = mybir.AxisListType

S = 2048
D = 1024
DI = 2816
NT = S // 128
ND = D // 128
NI = DI // 128
MAGIC = 12582912.0
EPS = 1e-5
QMAX = 127.0
INV127 = 1.0 / 127.0
SCALE_QK = 1.0 / 32.0


class B:

    ENGS = ("pe", "act", "dve", "pool", "sp")
    NQ = {"qsp": 12, "qgp": 8}

    def __init__(self):
        self.ops = {e: [] for e in B.ENGS}
        self.cnt = {e: 0 for e in B.ENGS}
        self.qidx = {q: 0 for q in B.NQ}
        self.qsemcnt = {f"{q}{i}": 0 for q in B.NQ for i in range(B.NQ[q])}
        self.sem = {}
        self.writer = {}
        self.readers = {}
        self.waited = {e: {} for e in B.ENGS}

    def _deps(self, reads, writes, extra):
        deps = []
        for r in reads:
            w = self.writer.get(r)
            if w:
                deps.append(w)
        for w in writes:
            ww = self.writer.get(w)
            if ww:
                deps.append(ww)
            deps.extend(self.readers.get(w, ()))
        deps.extend(extra)
        return deps

    def _emit(self, eng, fn, deps, inc_sem, tok):
        seq = []
        for (sname, val) in deps:
            if self.waited[eng].get(sname, 0) >= val:
                continue
            seq.append((sname, val))
            self.waited[eng][sname] = val
        sems = self.sem
        inc_val = 16 if inc_sem.startswith("q") else 1
        assert not (inc_sem.startswith("q") and inc_val != 16)

        def closure(e, seq=seq, fn=fn, inc_sem=inc_sem, inc_val=inc_val):
            for (sname, val) in seq:
                e.wait_ge(sems[sname], val)
            ins = fn(e)
            if ins is not None:
                ins.then_inc(sems[inc_sem], inc_val)
        self.ops[eng].append(closure)
        return tok

    def op(self, eng, fn, reads=(), writes=(), accum=(), extra=(), noinc=False,
           reg_reads=()):
        if noinc:
            def closure(e, fn=fn):
                fn(e)
            self.ops[eng].append(closure)
            return None
        deps = self._deps(reads, writes, extra)
        self.cnt[eng] += 1
        tok = (eng, self.cnt[eng])
        for w in writes:
            self.writer[w] = tok
            self.readers[w] = []
        for a in accum:
            self.writer[a] = tok
        for r in reads:
            self.readers.setdefault(r, []).append(tok)
        for r in reg_reads:
            self.readers.setdefault(r, []).append(tok)
        return self._emit(eng, fn, deps, eng, tok)

    def gdeps(self, keys):
        out = []
        for k in keys:
            w = self.writer.get(k)
            if w:
                out.append(w)
        return out

    def dma(self, queue, fn, reads=(), writes=(), extra=()):
        eng = "sp" if queue == "qsp" else "pool"
        deps = self._deps(reads, writes, extra)
        sname = f"{queue}{self.qidx[queue] % B.NQ[queue]}"
        self.qidx[queue] += 1
        if self.qsemcnt[sname] > 0:
            deps = deps + [(sname, self.qsemcnt[sname])]
        self.qsemcnt[sname] += 16
        tok = (sname, self.qsemcnt[sname])
        for w in writes:
            self.writer[w] = tok
            self.readers[w] = []
        for r in reads:
            self.readers.setdefault(r, []).append(tok)
        return self._emit(eng, fn, deps, sname, tok)

    def snap(self):
        toks = [(e, self.cnt[e]) for e in B.ENGS if self.cnt[e] > 0]
        toks += [(s, c) for s, c in self.qsemcnt.items() if c > 0]
        return toks

    def run(self, nc):
        with ExitStack() as ctx:
            for e in B.ENGS:
                self.sem[e] = ctx.enter_context(nc.semaphore(f"sem_{e}"))
            for s in self.qsemcnt:
                self.sem[s] = ctx.enter_context(nc.semaphore(f"sem_{s}"))
            block = ctx.enter_context(nc.Block())
            prog = self

            @block.tensor
            def _(t):
                for f in prog.ops["pe"]:
                    f(t)

            @block.scalar
            def _(s):
                for f in prog.ops["act"]:
                    f(s)

            @block.vector
            def _(v):
                for f in prog.ops["dve"]:
                    f(v)

            @block.gpsimd
            def _(g):
                for f in prog.ops["pool"]:
                    f(g)

            @block.sync
            def _(sy):
                for f in prog.ops["sp"]:
                    f(sy)
                for s, c in prog.qsemcnt.items():
                    if c > 0:
                        sy.wait_ge(prog.sem[s], c)


def build_nc():
    nc = bass.Bass("TRN2", target_bir_lowering=False)

    x_d = nc.dram_tensor("x", [S, D], F32, kind="ExternalInput")
    wq_d = nc.dram_tensor("wq", [D, D], F32, kind="ExternalInput")
    wk_d = nc.dram_tensor("wk", [D, D], F32, kind="ExternalInput")
    wv_d = nc.dram_tensor("wv", [D, D], F32, kind="ExternalInput")
    bq_d = nc.dram_tensor("bq", [1, D], F32, kind="ExternalInput")
    bk_d = nc.dram_tensor("bk", [1, D], F32, kind="ExternalInput")
    bv_d = nc.dram_tensor("bv", [1, D], F32, kind="ExternalInput")
    cx_d = nc.dram_tensor("clipx", [1, 1], F32, kind="ExternalInput")
    cd_d = nc.dram_tensor("clipd", [1, 1], F32, kind="ExternalInput")
    wg_d = nc.dram_tensor("wg", [DI, D], F32, kind="ExternalInput")
    wu_d = nc.dram_tensor("wu", [DI, D], F32, kind="ExternalInput")
    wd_d = nc.dram_tensor("wd", [D, DI], F32, kind="ExternalInput")
    out_d = nc.dram_tensor("out", [S, D], F32, kind="ExternalOutput")

    cc1i = nc.dram_tensor("cc1i", [128, 1], F32)
    cc1o = nc.dram_tensor("cc1o", [128, 1], F32)
    cc2i = nc.dram_tensor("cc2i", [128, 1], F32)
    cc2o = nc.dram_tensor("cc2o", [128, 1], F32)
    cc1bi = nc.dram_tensor("cc1bi", [128, 1], F32)
    cc1bo = nc.dram_tensor("cc1bo", [128, 1], F32)
    cc2bi = nc.dram_tensor("cc2bi", [128, 1], F32)
    cc2bo = nc.dram_tensor("cc2bo", [128, 1], F32)
    sb1_d = nc.dram_tensor("sbc1", [1, 2], F32)
    sb2_d = nc.dram_tensor("sbc2", [1, 2], F32)

    p = B()

    with ExitStack() as ctx:
        e = ctx.enter_context
        XR = e(nc.sbuf_tensor("XR", [128, NT * D], BF16))
        FM = e(nc.sbuf_tensor("FM", [128, ND * S], BF16))
        AR2 = e(nc.sbuf_tensor("AR2", [128, 49152], BF16))
        SCF = e(nc.sbuf_tensor("SCF", [128, 2816], F32))
        OST = e(nc.sbuf_tensor("OST", [128, 1024], F32))
        PTS = e(nc.sbuf_tensor("PTS", [128, 2048], BF16))
        QTS = e(nc.sbuf_tensor("QTS", [128, 2048], BF16))
        YST = e(nc.sbuf_tensor("YST", [128, 1024], BF16))
        BIA = e(nc.sbuf_tensor("BIA", [1, 3 * D], BF16))
        ONE = e(nc.sbuf_tensor("ONE", [1, 512], BF16))
        IDN = e(nc.sbuf_tensor("IDN", [128, 128], BF16))
        ST = e(nc.sbuf_tensor("ST", [128, 272], F32))
        SCL = e(nc.sbuf_tensor("SCL", [1, 64], F32))
        BNC = e(nc.sbuf_tensor("BNC", [1, 128], F32))
        BNC2 = e(nc.sbuf_tensor("BNC2", [1, 128], F32))
        WDQB = e(nc.sbuf_tensor("WDQB", [128, 2816], BF16))
        YST2 = e(nc.sbuf_tensor("YST2", [128, 1024], BF16))
        WDQC = e(nc.sbuf_tensor("WDQC", [128, 2816], BF16))

        pa = e(nc.psum_tensor("pa", [128, 2048], F32))
        pb = e(nc.psum_tensor("pb", [128, 1024], F32))
        pt = e(nc.psum_tensor("pt", [128, 1024], BF16))
        pu = e(nc.psum_tensor("pu", [128, 512], F32))

        AR2f = AR2.bitcast(F32)
        OSTb = OST.bitcast(BF16)
        YSTf = YST.bitcast(F32)
        YST2f = YST2.bitcast(F32)
        HQS = AR2f[:, 22528:24576]
        FM3 = FM.reshape([128, ND, S])
        pt3 = pt.reshape([128, 8, 128])

        C_RSTD1, C_NEGB1 = 0, 16
        C_RSTD2, C_NEGB2 = 32, 48
        C_COLMAX = 64
        C_UCOL = 80
        C_MV = 104
        C_ST6 = 106
        C_NEGM = 118
        C_RS = 119
        C_WSC = 122
        C_S1B, C_RS1B = 123, 124
        C_SDB, C_RSDB = 125, 126
        C_C32 = 127
        C_WGS = 128
        C_WUS = 150
        C_STD = 172
        C_WDS = 173
        C_M128 = 174
        C_NEGMT = 175
        C_MV2 = 176
        C_WSC2 = 210
        C_ST6B = 216
        C_MVB = 228
        C_STDB = 230
        C_EPS = 231
        C_PMX = 232
        C_RSP = 236
        C_M128B = 244
        C_LNT = 245

        L_CX, L_CD, L_GMAX, L_SIGX, L_S1, L_RS1 = 0, 1, 2, 3, 4, 5
        L_UMAX, L_SIGD, L_SD, L_RSD, L_LMAX = 6, 7, 8, 9, 10
        C_CXB, C_SGXB, C_GXB, C_LXB = 264, 266, 268, 269

        nc.const_aps.register(F32, SCALE_QK, ST[:, C_C32:C_C32 + 1])
        nc.const_aps.register(F32, EPS, ST[:, C_EPS:C_EPS + 1])
        nc.const_aps.register(F32, -0.5, ST[:, C_LNT + 17:C_LNT + 18])

        p.op("pool", lambda g: g.memset(ST[:, C_C32:C_C32 + 1], SCALE_QK),
             writes=[("st32",)])
        p.op("pool", lambda g: g.memset(ST[:, C_EPS:C_EPS + 1], EPS),
             writes=[("steps",)])
        p.op("pool", lambda g: g.memset(ST[:, C_LNT + 17:C_LNT + 18], -0.5),
             writes=[("stmh",)])
        p.op("pool", lambda g: g.memset(ONE[:], 1.0), writes=[("ONE",)])
        p.op("pool", lambda g: g.memset(IDN[:], 0.0), writes=[("IDN",)])
        p.op("pool", lambda g: g.affine_select(
            out=IDN[:], in_=IDN[:], compare_op=ALU.not_equal, fill=1.0,
            base=0, pattern=[[-1, 128]], channel_multiplier=1),
            reads=[("IDN",)], writes=[("IDN",)])

        WQv = AR2[:, 32768:40960].rearrange("p (c n) -> p c n", c=ND)
        WKVv = AR2[:, 40960:49152].rearrange("p (c n) -> p c n", c=ND)
        wq_src = wq_d.ap().rearrange("(c p) n -> p c n", p=128)
        wk_src = wk_d.ap().rearrange("(c p) n -> p c n", p=128)
        wv_src = wv_d.ap().rearrange("(c p) n -> p c n", p=128)
        XRv = XR.rearrange("p (r n) -> p r n", r=NT)
        x_src = x_d.ap().rearrange("(r p) n -> p r n", p=128)

        p.dma("qgp", lambda g: g.dma_start(out=WKVv, in_=wv_src), writes=[("WKV",)])
        p.dma("qgp", lambda g: g.dma_start(
            out=XRv[:, 0:1, :], in_=x_src[:, 0:1, :]), writes=[("XR", 0)])
        p.dma("qgp", lambda g: g.dma_start(
            out=XRv[:, 1:4, :], in_=x_src[:, 1:4, :]),
            writes=[("XR", r) for r in range(1, 4)])
        p.dma("qgp", lambda g: g.dma_start(out=SCL[0:1, L_CX:L_CX + 1], in_=cx_d.ap()),
              writes=[("scl", L_CX)])
        p.dma("qgp", lambda g: g.dma_start(out=SCL[0:1, L_CD:L_CD + 1], in_=cd_d.ap()),
              writes=[("scl", L_CD)])
        p.dma("qgp", lambda g: g.dma_start(out=BIA[0:1, 0:D], in_=bq_d.ap()),
              writes=[("bq",)])
        p.dma("qgp", lambda g: g.dma_start(out=BIA[0:1, D:2 * D], in_=bk_d.ap()),
              writes=[("bk",)])
        p.dma("qgp", lambda g: g.dma_start(out=BIA[0:1, 2 * D:3 * D], in_=bv_d.ap()),
              writes=[("bv",)])
        for i in range(1, 4):
            p.dma("qgp", lambda g, i=i: g.dma_start(
                out=XRv[:, 4 * i:4 * (i + 1), :], in_=x_src[:, 4 * i:4 * (i + 1), :]),
                writes=[("XR", r) for r in range(4 * i, 4 * i + 4)])
        p.dma("qgp", lambda g: g.dma_start(out=WQv, in_=wq_src), writes=[("WQ",)])

        p.dma("qsp", lambda sy: sy.dma_start(out=sb1_d.ap(), in_=SCL[0:1, 0:2]),
              reads=[("scl", L_CX), ("scl", L_CD)], writes=[("cbd",)])
        p.dma("qsp", lambda sy: sy.dma_start(out=ST[:, C_CXB:C_CXB + 2], in_=sb1_d.broadcast_to([128, 2])),
              reads=[("cbd",)], writes=[("cxb",)])
        p.op("act", lambda s: s.activation(out=ST[:, C_SGXB:C_SGXB + 2], in_=ST[:, C_CXB:C_CXB + 2], func=AF.Sigmoid),
             reads=[("cxb",)], writes=[("sgxb",)])

        def xr_tile(r):
            return XR[:, r * D:(r + 1) * D]

        def ln1_tile(r):
            par = r % 2
            c6 = C_ST6 if par == 0 else C_ST6B
            cmv = C_MV if par == 0 else C_MVB
            cstd = C_STD if par == 0 else C_STDB
            p.op("dve", lambda v, r=r, c6=c6: v.bn_stats(ST[:, c6:c6 + 6], xr_tile(r)[:, 0:512]),
                 reads=[("XR", r)], writes=[("st6a", par)])
            p.op("dve", lambda v, r=r, c6=c6: v.bn_stats(ST[:, c6 + 6:c6 + 12], xr_tile(r)[:, 512:1024]),
                 reads=[("XR", r)], writes=[("st6b", par)])
            p.op("dve", lambda v, c6=c6, cmv=cmv: v.bn_aggr(ST[:, cmv:cmv + 2], ST[:, c6:c6 + 12]),
                 reads=[("st6a", par), ("st6b", par)], writes=[("mv", par)])
            p.op("act", lambda s, cmv=cmv, cstd=cstd: s.activation(
                out=ST[:, cstd:cstd + 1], in_=ST[:, cmv + 1:cmv + 2], func=AF.Sqrt, bias=EPS),
                reads=[("mv", par), ("steps",)], writes=[("lstd", par)])
            p.op("dve", lambda v, r=r, cstd=cstd: v.reciprocal(ST[:, C_RSTD1 + r:C_RSTD1 + r + 1], ST[:, cstd:cstd + 1]),
                 reads=[("lstd", par)], writes=[("rstd1", r)])
            p.op("dve", lambda v, r=r, cmv=cmv: v.tensor_scalar_mul(
                ST[:, C_NEGB1 + r:C_NEGB1 + r + 1], ST[:, cmv:cmv + 1], -1.0),
                reads=[("mv", par)], writes=[("negb1", r)])
            yb = YST if r % 2 == 0 else YST2
            ky = ("YST", r % 2)
            p.op("dve", lambda v, r=r, yb=yb: v.tensor_scalar(
                yb[:], xr_tile(r), ST[:, C_NEGB1 + r:C_NEGB1 + r + 1],
                ST[:, C_RSTD1 + r:C_RSTD1 + r + 1], ALU.add, ALU.mult),
                reads=[("XR", r), ("rstd1", r), ("negb1", r)], writes=[ky])
            u = p.gdeps([ky, ("IDN",)])
            for c in range(ND):
                p.op("pe", lambda t, c=c, yb=yb: t.transpose(
                    pt[:, c * 128:(c + 1) * 128], yb[:, c * 128:(c + 1) * 128], IDN[:]),
                    writes=[("pt", c)] if c == 0 else (), extra=u if c == 0 else (),
                    noinc=(0 < c < ND - 1),
                    accum=[("pt", cc) for cc in range(1, ND)] if c == ND - 1 else (),
                    reg_reads=[ky, ("IDN",)] if c == ND - 1 else ())
            p.op("dve", lambda v, r=r: v.tensor_copy(
                FM3[:, :, r * 128:(r + 1) * 128], pt3[:, :, :]),
                reads=[("pt", 0), ("pt", ND - 1)], writes=[("FM", r)])

        KTv = AR2[:, 0:16384].rearrange("p (c n) -> p c n", c=ND)
        VTv = AR2[:, 16384:32768].rearrange("p (c n) -> p c n", c=NT)
        gctr = [0]

        def v_group(tc, h):
            bank = ("pb", h)
            keys = [("WKV",), ("bv",), ("ONE",), ("FM", tc)]
            u = p.gdeps(keys)
            p.op("pe", lambda t, tc=tc, h=h: t.matmul(
                pb[:, h * 512:(h + 1) * 512], FM3[:, 0, tc * 128:(tc + 1) * 128],
                WKVv[:, 0, h * 512:(h + 1) * 512], start=True, stop=False),
                writes=[bank], extra=u)
            for dc in range(1, ND):
                p.op("pe", lambda t, tc=tc, h=h, dc=dc: t.matmul(
                    pb[:, h * 512:(h + 1) * 512], FM3[:, dc, tc * 128:(tc + 1) * 128],
                    WKVv[:, dc, h * 512:(h + 1) * 512], start=False, stop=False), noinc=True)
            p.op("pe", lambda t, tc=tc, h=h: t.matmul(
                pb[:, h * 512:(h + 1) * 512], ONE[0:1, 0:128],
                BIA[0:1, 2 * D + h * 512:2 * D + (h + 1) * 512], start=False, stop=True),
                accum=[bank], reg_reads=keys)
            p.op("act", lambda s, tc=tc, h=h: s.activation(
                out=VTv[:, tc, h * 512:(h + 1) * 512], in_=pb[:, h * 512:(h + 1) * 512], func=AF.Copy),
                reads=[bank], writes=[("VT", tc, h)])

        def k_group(c, j):
            b = gctr[0] % 4
            gctr[0] += 1
            bank = ("pa", b)
            keys = [("WKV2",)] + [("FM", rr) for rr in range(4 * j, 4 * j + 4)]
            u = p.gdeps(keys)
            p.op("pe", lambda t, c=c, j=j, b=b: t.matmul(
                pa[:, b * 512:(b + 1) * 512], WKVv[:, 0, c * 128:(c + 1) * 128],
                FM3[:, 0, j * 512:(j + 1) * 512], start=True, stop=False),
                writes=[bank], extra=u)
            for dc in range(1, ND - 1):
                p.op("pe", lambda t, c=c, j=j, dc=dc, b=b: t.matmul(
                    pa[:, b * 512:(b + 1) * 512], WKVv[:, dc, c * 128:(c + 1) * 128],
                    FM3[:, dc, j * 512:(j + 1) * 512], start=False, stop=False), noinc=True)
            p.op("pe", lambda t, c=c, j=j, b=b: t.matmul(
                pa[:, b * 512:(b + 1) * 512], WKVv[:, ND - 1, c * 128:(c + 1) * 128],
                FM3[:, ND - 1, j * 512:(j + 1) * 512], start=False, stop=True),
                accum=[bank], reg_reads=keys)
            p.op("act", lambda s, c=c, j=j, b=b: s.activation(
                out=KTv[:, c, j * 512:(j + 1) * 512], in_=pa[:, b * 512:(b + 1) * 512], func=AF.Copy),
                reads=[bank], writes=[("KT", c, j)])

        for r in range(NT):
            ln1_tile(r)
            v_group(r, 0)
            v_group(r, 1)

        qts_banks = [(pu, 0, ("pu",)), (pb, 0, ("pb", 0)), (pb, 512, ("pb", 1))]

        def qts_buf(c):
            return (QTS, c * 512) if c < 4 else (OSTb, (c - 4) * 512)

        def emit_qts(q):
            for c in range(ND):
                bt, bo, bkey = qts_banks[c % 3]
                keys = [("WQ",), ("bq",), ("ONE",)] + [("FM", 4 * q + z) for z in range(4)]
                u = p.gdeps(keys)
                p.op("pe", lambda t, c=c, q=q, bt=bt, bo=bo: t.matmul(
                    bt[:, bo:bo + 512], WQv[:, 0, c * 128:(c + 1) * 128],
                    FM3[:, 0, q * 512:(q + 1) * 512], start=True, stop=False),
                    writes=[bkey], extra=u)
                for dc in range(1, ND):
                    p.op("pe", lambda t, c=c, q=q, dc=dc, bt=bt, bo=bo: t.matmul(
                        bt[:, bo:bo + 512], WQv[:, dc, c * 128:(c + 1) * 128],
                        FM3[:, dc, q * 512:(q + 1) * 512], start=False, stop=False), noinc=True)
                p.op("pe", lambda t, c=c, bt=bt, bo=bo: t.matmul(
                    bt[:, bo:bo + 512], BIA[0:1, c * 128:(c + 1) * 128],
                    ONE[0:1, 0:512], start=False, stop=True),
                    accum=[bkey], reg_reads=keys)
                qb, qo = qts_buf(c)
                p.op("act", lambda s, c=c, bt=bt, bo=bo, qb=qb, qo=qo: s.activation(
                    out=qb[:, qo:qo + 512], in_=bt[:, bo:bo + 512],
                    func=AF.Copy, scale=SCALE_QK),
                    reads=[bkey, ("st32",)], writes=[("QTS", c)])

        emit_qts(0)

        p.dma("qgp", lambda g: g.dma_start(out=WKVv, in_=wk_src),
              writes=[("WKV",), ("WKV2",)])
        for j in range(4):
            for c in range(ND):
                k_group(c, j)

        Psb0 = AR2[:, 40960:43008]
        Psb1 = AR2[:, 43008:45056]
        fence_psb = p.snap()

        def ln2_stats(i):
            par = i % 2
            c6 = C_ST6 if par == 0 else C_ST6B
            p.op("dve", lambda v, i=i, c6=c6: v.bn_stats(ST[:, c6:c6 + 6], xr_tile(i)[:, 0:512]),
                 reads=[("XR", i)], writes=[("st6a", par)])
            p.op("dve", lambda v, i=i, c6=c6: v.bn_stats(ST[:, c6 + 6:c6 + 12], xr_tile(i)[:, 512:1024]),
                 reads=[("XR", i)], writes=[("st6b", par)])
            p.op("dve", lambda v, i=i, c6=c6: v.bn_aggr(ST[:, C_MV2 + 2 * i:C_MV2 + 2 * i + 2], ST[:, c6:c6 + 12]),
                 reads=[("st6a", par), ("st6b", par)], writes=[("mv2", i)])
            hs = HQS[:, (i % 2) * 1024:(i % 2) * 1024 + 1024]
            p.op("dve", lambda v, i=i: v.tensor_scalar_mul(
                ST[:, C_NEGMT:C_NEGMT + 1], ST[:, C_MV2 + 2 * i:C_MV2 + 2 * i + 1], -1.0),
                reads=[("mv2", i)], writes=[("negmt",)])
            p.op("act", lambda s, i=i, hs=hs: s.activation(
                out=hs, in_=xr_tile(i), func=AF.Abs, bias=ST[:, C_NEGMT:C_NEGMT + 1]),
                reads=[("XR", i), ("negmt",)], writes=[("hqs", i % 2)])
            p.op("dve", lambda v, i=i, hs=hs: v.reduce_max(
                ST[:, C_COLMAX + i:C_COLMAX + i + 1], hs, axis=AX.X),
                reads=[("hqs", i % 2)], writes=[("colmax", i)])

        def psb(i):
            return Psb0 if i % 2 == 0 else Psb1

        def emit_scores(i):
            sub = i % 4
            def qsl(c):
                qb, qo = qts_buf(c)
                return qb[:, qo + sub * 128:qo + sub * 128 + 128]
            for j in range(4):
                bank = ("pa", j)
                keys = [("QTS", c) for c in range(ND)] + [("KT", c, j) for c in range(ND)]
                u = p.gdeps(keys)
                p.op("pe", lambda t, j=j, a=qsl(0): t.matmul(
                    pa[:, j * 512:(j + 1) * 512], a,
                    KTv[:, 0, j * 512:(j + 1) * 512], start=True, stop=False),
                    writes=[bank], extra=u)
                for c in range(1, ND - 1):
                    p.op("pe", lambda t, j=j, a=qsl(c), c=c: t.matmul(
                        pa[:, j * 512:(j + 1) * 512], a,
                        KTv[:, c, j * 512:(j + 1) * 512], start=False, stop=False), noinc=True)
                p.op("pe", lambda t, j=j, a=qsl(ND - 1): t.matmul(
                    pa[:, j * 512:(j + 1) * 512], a,
                    KTv[:, ND - 1, j * 512:(j + 1) * 512], start=False, stop=True),
                    accum=[bank], reg_reads=keys)
                p.op("dve", lambda v, j=j: v.reduce_max(
                    ST[:, C_PMX + j:C_PMX + j + 1], pa[:, j * 512:(j + 1) * 512], axis=AX.X),
                    reads=[bank], writes=[("pmx", j)])

        def emit_max(i):
            p.op("dve", lambda v: v.reduce_max(ST[:, C_NEGM:C_NEGM + 1],
                                               ST[:, C_PMX:C_PMX + 4], axis=AX.X, negate=True),
                 reads=[("pmx", j) for j in range(4)], writes=[("negm",)])

        def emit_softmax(i):
            par = i % 2
            for j in range(4):
                p.op("act", lambda s, i=i, j=j, par=par: s.activation(
                    out=psb(i)[:, j * 512:(j + 1) * 512], in_=pa[:, j * 512:(j + 1) * 512],
                    func=AF.Exp, bias=ST[:, C_NEGM:C_NEGM + 1],
                    accum_out=ST[:, C_RSP + 4 * par + j:C_RSP + 4 * par + j + 1]),
                    reads=[("pa", j), ("negm",)],
                    writes=[("Psb4", par, j), ("sumexp", par, j)], extra=fence_psb)
            p.op("dve", lambda v, par=par: v.reduce_sum(
                ST[:, C_RS + par:C_RS + par + 1], ST[:, C_RSP + 4 * par:C_RSP + 4 * par + 4], axis=AX.X),
                reads=[("sumexp", par, j) for j in range(4)], writes=[("sumt", par)])
            p.op("dve", lambda v, par=par: v.reciprocal(
                ST[:, C_RS + par:C_RS + par + 1], ST[:, C_RS + par:C_RS + par + 1]),
                reads=[("sumt", par)], writes=[("rs", par)])

        def emit_pv_head(i):
            for half in range(2):
                u = p.gdeps([("Psb4", i % 2, j) for j in range(4)] + [("IDN",)])
                for c in range(8):
                    tc = half * 8 + c
                    p.op("pe", lambda t, tc=tc, c=c, i=i: t.transpose(
                        pt[:, c * 128:(c + 1) * 128], psb(i)[:, tc * 128:(tc + 1) * 128], IDN[:]),
                        writes=[("pt", c)] if c == 0 else (), extra=u if c == 0 else (),
                        noinc=(0 < c < 7),
                        accum=[("pt", cc) for cc in range(1, 8)] if c == 7 else (),
                        reg_reads=[("Psb4", i % 2, j) for j in range(4)] + [("IDN",)] if c == 7 else ())
                p.op("act", lambda s, half=half: s.activation(
                    out=PTS[:, half * 1024:(half + 1) * 1024], in_=pt[:], func=AF.Copy),
                    reads=[("pt", 0), ("pt", 7)], writes=[("PTs", half)])
            for h in range(2):
                bank = ("pb", h)
                keys = [("PTs", 0), ("PTs", 1)] + [("VT", tc, h) for tc in range(NT)]
                u = p.gdeps(keys)
                p.op("pe", lambda t, h=h: t.matmul(
                    pb[:, h * 512:(h + 1) * 512], PTS[:, 0:128],
                    VTv[:, 0, h * 512:(h + 1) * 512], start=True, stop=False),
                    writes=[bank], extra=u)
                for tc in range(1, NT - 1):
                    p.op("pe", lambda t, h=h, tc=tc: t.matmul(
                        pb[:, h * 512:(h + 1) * 512], PTS[:, tc * 128:(tc + 1) * 128],
                        VTv[:, tc, h * 512:(h + 1) * 512], start=False, stop=False), noinc=True)
                p.op("pe", lambda t, h=h: t.matmul(
                    pb[:, h * 512:(h + 1) * 512], PTS[:, (NT - 1) * 128:NT * 128],
                    VTv[:, NT - 1, h * 512:(h + 1) * 512], start=False, stop=True),
                    accum=[bank], reg_reads=keys)
        def emit_pv_tail(i):
            p.op("dve", lambda v, i=i: v.scalar_tensor_tensor(
                xr_tile(i), pb[:], ST[:, C_RS + i % 2:C_RS + i % 2 + 1], xr_tile(i),
                ALU.mult, ALU.add),
                reads=[("pb", 0), ("pb", 1), ("rs", i % 2), ("XR", i)], writes=[("XR", i)])
            ln2_stats(i)

        def global_max_launch(cci, cco, tag, src_col, bdst, bkeys, defer_read=False,
                              send_queue="qgp"):
            p.dma(send_queue, lambda g, cci=cci, src_col=src_col: g.dma_start(
                out=cci.ap(), in_=ST[:, src_col:src_col + 1]),
                reads=[("m128", tag)], writes=[("cci", tag)])
            p.op("pool", lambda g, cci=cci, cco=cco: g.collective_compute(
                "AllReduce", ALU.max, replica_groups=[list(range(8))],
                ins=[cci.ap().opt()], outs=[cco.ap().opt()]),
                reads=[("cci", tag)], writes=[("cco", tag)])

            def emit_read():
                p.dma("qgp", lambda g, cco=cco, bdst=bdst: g.dma_start(
                    out=bdst, in_=cco.ap().rearrange("p one -> one p").broadcast_to([128, 128])),
                    reads=[("cco", tag)], writes=list(bkeys))
            if defer_read:
                return emit_read
            emit_read()

        def global_max_finish2(l_cell, bkey_a, bkey_b):
            p.op("dve", lambda v, l_cell=l_cell: v.reduce_max(
                SCL[0:1, l_cell:l_cell + 1], BNC[0:1, :], axis=AX.X),
                reads=[bkey_a], writes=[("gma", l_cell)])
            p.op("dve", lambda v, l_cell=l_cell: v.reduce_max(
                SCL[0:1, L_LMAX:L_LMAX + 1], BNC2[0:1, :], axis=AX.X),
                reads=[bkey_b], writes=[("gmb", l_cell)])
            p.op("dve", lambda v, l_cell=l_cell: v.tensor_tensor(
                SCL[0:1, l_cell:l_cell + 1], SCL[0:1, l_cell:l_cell + 1],
                SCL[0:1, L_LMAX:L_LMAX + 1], ALU.max),
                reads=[("gma", l_cell), ("gmb", l_cell)], writes=[("scl", l_cell)])

        def rstd2_batch(lo, hi, tag):
            n = hi - lo
            vv_ = ST[:, C_MV2 + 2 * lo:C_MV2 + 2 * hi].rearrange(
                "p (r two) -> p r two", two=2)[:, :, 1]
            mm_ = ST[:, C_MV2 + 2 * lo:C_MV2 + 2 * hi].rearrange(
                "p (r two) -> p r two", two=2)[:, :, 0]
            mv2k = [("mv2", i) for i in range(lo, hi)]
            p.op("act", lambda s, vv_=vv_, lo=lo, n=n: s.activation(
                out=ST[:, C_LNT + lo:C_LNT + lo + n], in_=vv_, func=AF.Ln, bias=EPS),
                reads=mv2k + [("steps",)], writes=[("lnt", tag)])
            p.op("act", lambda s, lo=lo, n=n: s.activation(
                out=ST[:, C_RSTD2 + lo:C_RSTD2 + lo + n],
                in_=ST[:, C_LNT + lo:C_LNT + lo + n], func=AF.Exp, scale=-0.5),
                reads=[("lnt", tag), ("stmh",)], writes=[("rstd2", tag)])
            p.op("dve", lambda v, mm_=mm_, lo=lo, n=n: v.tensor_scalar_mul(
                ST[:, C_NEGB2 + lo:C_NEGB2 + lo + n], mm_, -1.0),
                reads=mv2k, writes=[("negb2", tag)])
            p.op("dve", lambda v, lo=lo, n=n: v.tensor_tensor(
                ST[:, C_COLMAX + lo:C_COLMAX + lo + n], ST[:, C_COLMAX + lo:C_COLMAX + lo + n],
                ST[:, C_RSTD2 + lo:C_RSTD2 + lo + n], ALU.mult),
                reads=[("colmax", i) for i in range(lo, hi)] + [("rstd2", tag)],
                writes=[("colmaxn", tag)])

        pend = {}

        def emit_gmaxA():
            rstd2_batch(0, 13, "a")
            p.op("dve", lambda v: v.reduce_max(ST[:, C_M128:C_M128 + 1],
                                               ST[:, C_COLMAX:C_COLMAX + 13], axis=AX.X),
                 reads=[("colmaxn", "a")], writes=[("m128", "c1a")])
            pend["c1a"] = global_max_launch(cc1i, cc1o, "c1a", C_M128, YSTf[:, 0:128],
                                            [("bnca",), ("YST", 0)],
                                            defer_read=True, send_queue="qsp")

        for i in range(NT):
            if i % 4 == 0 and i > 0:
                emit_qts(i // 4)
            emit_scores(i)
            emit_max(i)
            if i > 0:
                emit_pv_head(i - 1)
            emit_softmax(i)
            if i > 0:
                emit_pv_tail(i - 1)
            if i == NT - 2:
                emit_gmaxA()
        emit_pv_head(NT - 1)
        emit_pv_tail(NT - 1)

        SG = AR2

        WINT4 = [WDQB[:, 0:1024], WDQB[:, 1024:2048],
                 QTS[:, 0:1024], QTS[:, 1024:2048]]
        WTB4 = [OSTb[:, 0:1024], OSTb[:, 1024:2048],
                PTS[:, 0:1024], PTS[:, 1024:2048]]

        STAGE4 = [SCF[:, 0:1024], SCF[:, 1024:2048],
                  HQS[:, 0:1024], HQS[:, 1024:2048]]

        def prep_quant(w_src_d, c_ws, phase, c):
                b4 = c % 4
                stage = STAGE4[b4]
                wint = WINT4[b4]
                kst = ("wstage", b4)
                kwi = ("wint", b4)
                src = w_src_d.ap()[c * 128:(c + 1) * 128, :]
                ex = FXQ if (phase == "g" and 2 <= c < 4) else ()
                p.dma("qsp", lambda sy, src=src, stage=stage: sy.dma_start(out=stage, in_=src),
                      writes=[kst], extra=ex)
                p.op("dve", lambda v, c=c, stage=stage, c_ws=c_ws: v.reduce_max(
                    ST[:, c_ws + c:c_ws + c + 1], stage, axis=AX.X, apply_absolute_value=True),
                    reads=[kst], writes=[("ws", phase, c)])
                p.op("dve", lambda v, c=c, c_ws=c_ws: v.tensor_scalar_mul(
                    ST[:, c_ws + c:c_ws + c + 1], ST[:, c_ws + c:c_ws + c + 1], INV127),
                    reads=[("ws", phase, c)], writes=[("ws", phase, c)])
                p.op("dve", lambda v, c=c, c_ws=c_ws: v.reciprocal(
                    ST[:, C_WSC:C_WSC + 1], ST[:, c_ws + c:c_ws + c + 1]),
                    reads=[("ws", phase, c)], writes=[("rws",)])
                p.op("dve", lambda v, stage=stage: v.tensor_scalar(
                    stage, stage, ST[:, C_WSC:C_WSC + 1], None, ALU.mult),
                    reads=[kst, ("rws",)], writes=[kst])
                p.op("dve", lambda v, stage=stage, wint=wint: v.tensor_scalar(
                    wint, stage, MAGIC, MAGIC, ALU.add, ALU.subtract),
                    reads=[kst], writes=[kwi],
                    extra=FXQ if (phase == "g" and 2 <= c < 4) else ())

        def prep_transp(phase, c):
                b4 = c % 4
                wint = WINT4[b4]
                wtb = WTB4[b4]
                kwi = ("wint", b4)
                kwt = ("wt", b4)
                u = p.gdeps([kwi, ("IDN",)])
                for h in range(ND):
                    p.op("pe", lambda t, h=h, wint=wint: t.transpose(
                        pt[:, h * 128:(h + 1) * 128], wint[:, h * 128:(h + 1) * 128], IDN[:]),
                        writes=[("pt", h)] if h == 0 else (), extra=u if h == 0 else (),
                        noinc=(0 < h < ND - 1),
                        accum=[("pt", hh) for hh in range(1, ND)] if h == ND - 1 else (),
                        reg_reads=[kwi, ("IDN",)] if h == ND - 1 else ())
                p.op("act", lambda s, wtb=wtb: s.activation(out=wtb, in_=pt[:], func=AF.Copy),
                     reads=[("pt", 0), ("pt", ND - 1)], writes=[kwt],
                     extra=FXQ if (phase == "g" and 2 <= c < 4) else (FA if (phase == "g" and c < 2) else ()))
        BANK6 = [("pa", 0), ("pa", 1), ("pa", 2), ("pa", 3), ("pb", 0), ("pb", 1)]

        def bank_ap(bk):
            return pa[:, bk[1] * 512:(bk[1] + 1) * 512] if bk[0] == "pa" else \
                   pb[:, bk[1] * 512:(bk[1] + 1) * 512]

        def ffn_mm(c_ws, phase, c):
                b4 = c % 4
                wtb = WTB4[b4]
                kwt = ("wt", b4)
                p.op("dve", lambda v, c=c, c_ws=c_ws: v.tensor_scalar(
                    ST[:, c_ws + c:c_ws + c + 1], ST[:, c_ws + c:c_ws + c + 1],
                    ST[:, C_S1B:C_S1B + 1], None, ALU.mult),
                    reads=[("ws", phase, c), ("s1b",)], writes=[("sv", phase, c)])
                for j in range(4):
                    bank = BANK6[gctr[0] % 6]
                    gctr[0] += 1
                    bap = bank_ap(bank)
                    keys = [kwt] + [("FM", rr) for rr in range(4 * j, 4 * j + 4)]
                    u = p.gdeps(keys)
                    p.op("pe", lambda t, j=j, wtb=wtb, bap=bap: t.matmul(
                        bap, wtb[:, 0:128],
                        FM3[:, 0, j * 512:(j + 1) * 512], start=True, stop=False),
                        writes=[bank], extra=u)
                    for h in range(1, ND - 1):
                        p.op("pe", lambda t, j=j, h=h, wtb=wtb, bap=bap: t.matmul(
                            bap, wtb[:, h * 128:(h + 1) * 128],
                            FM3[:, h, j * 512:(j + 1) * 512], start=False, stop=False), noinc=True)
                    p.op("pe", lambda t, j=j, wtb=wtb, bap=bap: t.matmul(
                        bap, wtb[:, (ND - 1) * 128:ND * 128],
                        FM3[:, ND - 1, j * 512:(j + 1) * 512], start=False, stop=True),
                        accum=[bank], reg_reads=keys)
                    sgt = SG[:, c * 2048 + j * 512:c * 2048 + (j + 1) * 512]
                    if phase == "g":
                        p.op("act", lambda s, c=c, sgt=sgt, c_ws=c_ws, bap=bap: s.activation(
                            out=sgt, in_=bap, func=AF.Silu,
                            scale=ST[:, c_ws + c:c_ws + c + 1]),
                            reads=[bank, ("sv", phase, c)], writes=[("SG", c, j)],
                            extra=FXQ)
                    else:
                        ups = YST[:, (j % 2) * 512:(j % 2) * 512 + 512]
                        p.op("act", lambda s, c=c, ups=ups, c_ws=c_ws, bap=bap: s.activation(
                            out=ups, in_=bap, func=AF.Copy,
                            scale=ST[:, c_ws + c:c_ws + c + 1]),
                            reads=[bank, ("sv", phase, c)], writes=[("ups", j % 2)])
                        p.op("dve", lambda v, sgt=sgt, ups=ups, c=c, j=j: v.tensor_tensor(
                            sgt, sgt, ups, ALU.mult),
                            reads=[("ups", j % 2), ("SG", c, j)], writes=[("SG", c, j)])
                if phase == "u":
                    p.op("dve", lambda v, c=c: v.reduce_max(
                        ST[:, C_UCOL + c:C_UCOL + c + 1], SG[:, c * 2048:(c + 1) * 2048],
                        axis=AX.X, apply_absolute_value=True),
                        reads=[("SG", c, j) for j in range(4)], writes=[("ucol", c)])
                    if c == 15:
                        p.op("dve", lambda v: v.reduce_max(
                            ST[:, C_M128:C_M128 + 1], ST[:, C_UCOL:C_UCOL + 16], axis=AX.X),
                            reads=[("ucol", cc) for cc in range(16)], writes=[("m128", "c2a")])
                        global_max_launch(cc2i, cc2o, "c2a", C_M128, YST2f[:, 0:128],
                                          [("bnca2",), ("YST", 1)])

        def ffn_proj(w_src_d, c_ws, phase, skip_pre=0, tail_hook=None):
            for c in range(skip_pre, 3):
                prep_quant(w_src_d, c_ws, phase, c)
            for c in range(skip_pre, 2):
                prep_transp(phase, c)
            for c in range(NI):
                if c + 3 < NI:
                    prep_quant(w_src_d, c_ws, phase, c + 3)
                ffn_mm(c_ws, phase, c)
                if c + 2 < NI:
                    prep_transp(phase, c + 2)
                if tail_hook is not None:
                    tail_hook(c)


        FA = p.snap()

        rstd2_batch(13, 16, "b")
        p.op("dve", lambda v: v.reduce_max(ST[:, C_M128B:C_M128B + 1],
                                           ST[:, C_COLMAX + 13:C_COLMAX + 16], axis=AX.X),
             reads=[("colmaxn", "b")], writes=[("m128", "c1b")])
        read1b = global_max_launch(cc1bi, cc1bo, "c1b", C_M128B, YST2f[:, 0:128],
                                   [("bncb",), ("YST", 1)],
                                   defer_read=True, send_queue="qsp")
        pend["c1a"]()
        read1b()
        prep_quant(wg_d, C_WGS, "g", 0)
        prep_quant(wg_d, C_WGS, "g", 1)
        prep_transp("g", 0)
        prep_transp("g", 1)

        p.op("dve", lambda v: v.reduce_max(ST[:, C_GXB:C_GXB + 1], YSTf[:, 0:128], axis=AX.X),
             reads=[("bnca",), ("YST", 0)], writes=[("gmxa",)])
        p.op("dve", lambda v: v.reduce_max(ST[:, C_LXB:C_LXB + 1], YST2f[:, 0:128], axis=AX.X),
             reads=[("bncb",), ("YST", 1)], writes=[("gmxb",)])
        p.op("dve", lambda v: v.tensor_tensor(ST[:, C_GXB:C_GXB + 1], ST[:, C_GXB:C_GXB + 1],
                                              ST[:, C_LXB:C_LXB + 1], ALU.max),
             reads=[("gmxa",), ("gmxb",)], writes=[("gmx",)])
        p.op("dve", lambda v: v.tensor_tensor(ST[:, C_S1B:C_S1B + 1], ST[:, C_GXB:C_GXB + 1],
                                              ST[:, C_SGXB:C_SGXB + 1], ALU.mult),
             reads=[("gmx",), ("sgxb",)], writes=[("s1b",)])
        p.op("dve", lambda v: v.tensor_scalar_mul(ST[:, C_S1B:C_S1B + 1], ST[:, C_S1B:C_S1B + 1], INV127),
             reads=[("s1b",)], writes=[("s1b",)])
        p.op("dve", lambda v: v.reciprocal(ST[:, C_RS1B:C_RS1B + 1], ST[:, C_S1B:C_S1B + 1]),
             reads=[("s1b",)], writes=[("rs1b",)])
        p.op("dve", lambda v: v.tensor_scalar(
            ST[:, C_RSTD2:C_RSTD2 + 16], ST[:, C_RSTD2:C_RSTD2 + 16],
            ST[:, C_RS1B:C_RS1B + 1], None, ALU.mult),
            reads=[("rstd2", "a"), ("rstd2", "b"), ("rs1b",)], writes=[("rstd2q",)])

        for r in range(NT):
            hq = ("hqs", r % 2)
            hs = HQS[:, (r % 2) * 1024:(r % 2) * 1024 + 1024]
            p.op("dve", lambda v, r=r, hs=hs: v.tensor_scalar(
                hs, xr_tile(r), ST[:, C_NEGB2 + r:C_NEGB2 + r + 1],
                ST[:, C_RSTD2 + r:C_RSTD2 + r + 1], ALU.add, ALU.mult),
                reads=[("XR", r), ("negb2", "a"), ("negb2", "b"), ("rstd2q",)], writes=[hq])
            p.op("dve", lambda v, hs=hs: v.tensor_scalar(
                hs, hs, MAGIC, MAGIC, ALU.add, ALU.subtract),
                reads=[hq], writes=[hq])
            yb = YST if r % 2 == 0 else YST2
            ky = ("YST", r % 2)
            p.op("dve", lambda v, hs=hs, yb=yb: v.tensor_scalar(
                yb[:], hs, QMAX, -128.0, ALU.min, ALU.max),
                reads=[hq], writes=[ky])
            u = p.gdeps([ky, ("IDN",)])
            for c in range(ND):
                p.op("pe", lambda t, c=c, yb=yb: t.transpose(
                    pt[:, c * 128:(c + 1) * 128], yb[:, c * 128:(c + 1) * 128], IDN[:]),
                    writes=[("pt", c)] if c == 0 else (), extra=u if c == 0 else (),
                    noinc=(0 < c < ND - 1),
                    accum=[("pt", cc) for cc in range(1, ND)] if c == ND - 1 else (),
                    reg_reads=[ky, ("IDN",)] if c == ND - 1 else ())
            p.op("act", lambda s, r=r: s.activation(
                out=FM3[:, :, r * 128:(r + 1) * 128], in_=pt3[:, :, :], func=AF.Copy),
                reads=[("pt", 0), ("pt", ND - 1)], writes=[("FM", r)], extra=FA)

        FXQ = p.snap()

        WDT = FM
        WDTv = FM[:, 0:11264].rearrange("p (c n) -> p c n", c=NI)
        WDQ_A = FM[:, 11264:14080]
        WDQ_B = WDQB
        WDQ_C = WDQC

        def wd_quant(c, WDQ, magic_extra=(), magic_writes=()):
            stage = SCF[:, 0:2816]
            src = wd_d.ap()[c * 128:(c + 1) * 128, :]
            kq = ("wdq", c % 3)
            dw = [("wdstage",)] + ([("wstage", 0), ("wstage", 1)] if c == 0 else [])
            p.dma("qgp", lambda g, src=src, stage=stage: g.dma_start(out=stage, in_=src),
                  writes=dw)
            p.op("dve", lambda v, stage=stage: v.reduce_max(
                ST[:, C_WDS:C_WDS + 1], stage, axis=AX.X, apply_absolute_value=True),
                reads=[("wdstage",)], writes=[("wds",)])
            p.op("dve", lambda v: v.tensor_scalar_mul(ST[:, C_WDS:C_WDS + 1], ST[:, C_WDS:C_WDS + 1], INV127),
                 reads=[("wds",)], writes=[("wds",)])
            p.op("dve", lambda v: v.reciprocal(ST[:, C_WSC2:C_WSC2 + 1], ST[:, C_WDS:C_WDS + 1]),
                 reads=[("wds",)], writes=[("rws2",)])
            p.op("act", lambda s, stage=stage: s.activation(
                out=stage, in_=stage, func=AF.Copy, scale=ST[:, C_WSC2:C_WSC2 + 1]),
                reads=[("wdstage",), ("rws2",)], writes=[("wdstage",)])
            p.op("dve", lambda v, stage=stage, WDQ=WDQ: v.tensor_scalar(
                WDQ[:], stage, MAGIC, MAGIC, ALU.add, ALU.subtract),
                reads=[("wdstage",)], writes=[kq] + list(magic_writes), extra=magic_extra)
            p.op("act", lambda s, WDQ=WDQ: s.activation(
                out=WDQ[:], in_=WDQ[:], func=AF.Copy, scale=ST[:, C_WDS:C_WDS + 1]),
                reads=[kq, ("wds",)], writes=[kq])

        def wd_transp(c, WDQ, fence=()):
            kq = ("wdq", c % 3)
            sub = WDTv[:, :, (c % 4) * 128:(c % 4) * 128 + 128]
            for i0 in range(0, NI, 8):
                n = min(8, NI - i0)
                u = p.gdeps([kq, ("IDN",)]) + (list(fence) if i0 == 0 else [])
                for i in range(n):
                    p.op("pe", lambda t, i=i, i0=i0, WDQ=WDQ: t.transpose(
                        pt[:, i * 128:(i + 1) * 128], WDQ[:, (i0 + i) * 128:(i0 + i + 1) * 128], IDN[:]),
                        writes=[("pt", i)] if i == 0 else (), extra=u if i == 0 else (),
                        noinc=(0 < i < n - 1),
                        accum=[("pt", ii) for ii in range(1, n)] if i == n - 1 else (),
                        reg_reads=[kq, ("IDN",)] if i == n - 1 else ())
                p.op("dve", lambda v, i0=i0, n=n, sub=sub: v.tensor_copy(
                    sub[:, i0:i0 + n, :], pt3[:, 0:n, :]),
                    reads=[("pt", 0), ("pt", n - 1)], writes=[("WDT", c % 4)])

        def wdq_of(c):
            return [WDQ_B, WDQ_C, WDQ_A][c % 3]

        def wd_tail_hook(c):
            if c == 19:
                wd_quant(0, wdq_of(0), magic_writes=[("wint", 0), ("wint", 1)])
            elif c == 20:
                wd_quant(1, wdq_of(1))

        ffn_proj(wg_d, C_WGS, "g", skip_pre=2)
        ffn_proj(wu_d, C_WUS, "u", tail_hook=wd_tail_hook)

        FW = p.snap()
        FWPE = [("pe", p.cnt["pe"])]

        p.op("dve", lambda v: v.reduce_max(ST[:, C_M128B:C_M128B + 1],
                                           ST[:, C_UCOL + 16:C_UCOL + NI], axis=AX.X),
             reads=[("ucol", c) for c in range(16, NI)], writes=[("m128", "c2b")])
        p.dma("qsp", lambda sy: sy.dma_start(out=cc2bi.ap(), in_=ST[:, C_M128B:C_M128B + 1]),
              reads=[("m128", "c2b")], writes=[("cci", "c2b")])
        wd_quant(2, wdq_of(2), magic_extra=FWPE)
        p.op("pool", lambda g: g.collective_compute(
            "AllReduce", ALU.max, replica_groups=[list(range(8))],
            ins=[cc2bi.ap().opt()], outs=[cc2bo.ap().opt()]),
            reads=[("cci", "c2b")], writes=[("cco", "c2b")])
        wd_transp(0, wdq_of(0), fence=FWPE)
        wd_transp(1, wdq_of(1), fence=FWPE)
        wd_transp(2, wdq_of(2), fence=FWPE)
        wd_quant(3, wdq_of(3))
        wd_transp(3, wdq_of(3), fence=FWPE)
        p.dma("qgp", lambda g: g.dma_start(
            out=YSTf[:, 0:128], in_=cc2bo.ap().rearrange("p one -> one p").broadcast_to([128, 128])),
            reads=[("cco", "c2b")], writes=[("bncb2",), ("ups", 0)])

        p.op("dve", lambda v: v.reduce_max(ST[:, C_GXB:C_GXB + 1], YST2f[:, 0:128], axis=AX.X),
             reads=[("bnca2",), ("YST", 1)], writes=[("gmxa",), ("gmx",)])
        p.op("dve", lambda v: v.reduce_max(ST[:, C_LXB:C_LXB + 1], YSTf[:, 0:128], axis=AX.X),
             reads=[("bncb2",), ("ups", 0)], writes=[("gmxb",)])
        p.op("dve", lambda v: v.tensor_tensor(ST[:, C_GXB:C_GXB + 1], ST[:, C_GXB:C_GXB + 1],
                                              ST[:, C_LXB:C_LXB + 1], ALU.max),
             reads=[("gmxa",), ("gmxb",)], writes=[("gmx",)])
        p.op("dve", lambda v: v.tensor_tensor(ST[:, C_SDB:C_SDB + 1], ST[:, C_GXB:C_GXB + 1],
                                              ST[:, C_SGXB + 1:C_SGXB + 2], ALU.mult),
             reads=[("gmx",), ("sgxb",)], writes=[("sdb",)])
        p.op("dve", lambda v: v.tensor_scalar_mul(ST[:, C_SDB:C_SDB + 1], ST[:, C_SDB:C_SDB + 1], INV127),
             reads=[("sdb",)], writes=[("sdb",)])
        p.op("dve", lambda v: v.reciprocal(ST[:, C_RSDB:C_RSDB + 1], ST[:, C_SDB:C_SDB + 1]),
             reads=[("sdb",)], writes=[("rsdb",)])

        hu = 0
        for j in range(2):
            for c in range(NI):
                par = hu % 2
                ex = FW if hu < 2 else ()
                hu += 1
                hq = ("hqs", par)
                hs = HQS[:, par * 1024:par * 1024 + 1024]
                sgs = SG[:, c * 2048 + j * 1024:c * 2048 + (j + 1) * 1024]
                rk = [("SG", c, 2 * j), ("SG", c, 2 * j + 1)]
                p.op("act", lambda s, hs=hs, sgs=sgs: s.activation(
                    out=hs, in_=sgs, func=AF.Copy, scale=ST[:, C_RSDB:C_RSDB + 1]),
                    reads=rk + [("rsdb",)], writes=[hq], extra=ex)
                p.op("dve", lambda v, hs=hs: v.tensor_scalar(
                    hs, hs, MAGIC, MAGIC, ALU.add, ALU.subtract),
                    reads=[hq], writes=[hq])
                p.op("dve", lambda v, hs=hs, sgs=sgs, rk=tuple(rk): v.tensor_scalar(
                    sgs, hs, QMAX, -128.0, ALU.min, ALU.max),
                    reads=[hq] + list(rk), writes=list(rk))

        out_v = out_d.ap().rearrange("(r p) n -> p r n", p=128)
        wdt_keys = [("WDT", q) for q in range(4)]

        def down_half(H, hook=None):
            for tb in range(NT):
                bank = BANK6[gctr[0] % 6]
                gctr[0] += 1
                bap = bank_ap(bank)
                keys = wdt_keys + [("SG", c, tb // 4) for c in range(NI)]
                trickle = (H == 0 and tb == 0)
                u = p.gdeps(wdt_keys + [("SG", 0, 0)]) if trickle else p.gdeps(keys)
                p.op("pe", lambda t, tb=tb, bap=bap: t.matmul(
                    bap,
                    SG[:, tb * 128:tb * 128 + 128],
                    WDTv[:, 0, :], start=True, stop=False),
                    writes=[bank], extra=u)
                for c in range(1, NI - 1):
                    p.op("pe", lambda t, tb=tb, c=c, bap=bap: t.matmul(
                        bap,
                        SG[:, c * 2048 + tb * 128:c * 2048 + tb * 128 + 128],
                        WDTv[:, c, :], start=False, stop=False),
                        noinc=not trickle,
                        reads=[("SG", c, 0)] if trickle else (),
                        accum=[bank] if trickle else ())
                p.op("pe", lambda t, tb=tb, bap=bap: t.matmul(
                    bap,
                    SG[:, (NI - 1) * 2048 + tb * 128:(NI - 1) * 2048 + tb * 128 + 128],
                    WDTv[:, NI - 1, :], start=False, stop=True),
                    reads=[("SG", NI - 1, 0)] if trickle else (),
                    accum=[bank], reg_reads=keys)
                xs = XR[:, tb * D + H * 512:tb * D + (H + 1) * 512]
                par = (tb + H) % 2
                ot = OST[:, par * 512:(par + 1) * 512]
                p.op("dve", lambda v, tb=tb, xs=xs, ot=ot, bap=bap: v.scalar_tensor_tensor(
                    ot, bap, ST[:, C_SDB:C_SDB + 1],
                    xs, ALU.mult, ALU.add),
                    reads=[bank, ("sdb",), ("XR", tb)], writes=[("XO", par)])
                p.dma("qsp", lambda sy, tb=tb, H=H, ot=ot: sy.dma_start(
                    out=out_v[:, tb, H * 512:(H + 1) * 512], in_=ot),
                    reads=[("XO", par)])
                if hook is not None:
                    hook(tb)

        def wd46_hook(tb):
            if tb == 0:
                wd_quant(4, wdq_of(4))
            elif tb == 3:
                wd_quant(5, wdq_of(5))
            elif tb == 7:
                wd_quant(6, wdq_of(6))

        down_half(0, hook=wd46_hook)
        wd_transp(4, wdq_of(4))
        wd_quant(7, wdq_of(7))
        wd_transp(5, wdq_of(5))
        wd_transp(6, wdq_of(6))
        wd_transp(7, wdq_of(7))
        down_half(1)

        p.run(nc)
    return nc


_NC_CACHE = {}


def kernel(L0_attn_X, L0_attn_W_q, L0_attn_W_k, L0_attn_W_v,
           L0_attn_b_q, L0_attn_b_k, L0_attn_b_v,
           L0_ffn_clip_x, L0_ffn_clip_down,
           L0_ffn_w_gate, L0_ffn_w_up, L0_ffn_w_down):
    if "nc" not in _NC_CACHE:
        _NC_CACHE["nc"] = build_nc()
    nc = _NC_CACHE["nc"]

    X = np.ascontiguousarray(np.asarray(L0_attn_X, dtype=np.float32))
    shared = {
        "wq": np.ascontiguousarray(np.asarray(L0_attn_W_q, dtype=np.float32)),
        "wk": np.ascontiguousarray(np.asarray(L0_attn_W_k, dtype=np.float32)),
        "wv": np.ascontiguousarray(np.asarray(L0_attn_W_v, dtype=np.float32)),
        "bq": np.asarray(L0_attn_b_q, dtype=np.float32).reshape(1, D),
        "bk": np.asarray(L0_attn_b_k, dtype=np.float32).reshape(1, D),
        "bv": np.asarray(L0_attn_b_v, dtype=np.float32).reshape(1, D),
        "clipx": np.asarray(L0_ffn_clip_x, dtype=np.float32).reshape(1, 1),
        "clipd": np.asarray(L0_ffn_clip_down, dtype=np.float32).reshape(1, 1),
        "wg": np.ascontiguousarray(np.asarray(L0_ffn_w_gate, dtype=np.float32)),
        "wu": np.ascontiguousarray(np.asarray(L0_ffn_w_up, dtype=np.float32)),
        "wd": np.ascontiguousarray(np.asarray(L0_ffn_w_down, dtype=np.float32)),
    }
    in_maps = [dict(shared, x=np.ascontiguousarray(X[c])) for c in range(8)]
    res = run_bass_kernel_spmd(nc, in_maps, core_ids=list(range(8)))
    out = np.stack([np.asarray(res.results[c]["out"]) for c in range(8)], axis=0)
    return out.astype(np.float32)


# version 27
# speedup vs baseline: 1.0193x; 1.0070x over previous
import numpy as np
import concourse.bass as bass
import concourse.mybir as mybir
from concourse.bass_utils import run_bass_kernel_spmd
from contextlib import ExitStack

F32 = mybir.dt.float32
BF16 = mybir.dt.bfloat16
AF = mybir.ActivationFunctionType
ALU = mybir.AluOpType
AX = mybir.AxisListType

S = 2048
D = 1024
DI = 2816
NT = S // 128
ND = D // 128
NI = DI // 128
MAGIC = 12582912.0
EPS = 1e-5
QMAX = 127.0
INV127 = 1.0 / 127.0
SCALE_QK = 1.0 / 32.0


class B:

    ENGS = ("pe", "act", "dve", "pool", "sp")
    NQ = {"qsp": 12, "qgp": 8}

    def __init__(self):
        self.ops = {e: [] for e in B.ENGS}
        self.cnt = {e: 0 for e in B.ENGS}
        self.qidx = {q: 0 for q in B.NQ}
        self.qsemcnt = {f"{q}{i}": 0 for q in B.NQ for i in range(B.NQ[q])}
        self.sem = {}
        self.writer = {}
        self.readers = {}
        self.waited = {e: {} for e in B.ENGS}

    def _deps(self, reads, writes, extra):
        deps = []
        for r in reads:
            w = self.writer.get(r)
            if w:
                deps.append(w)
        for w in writes:
            ww = self.writer.get(w)
            if ww:
                deps.append(ww)
            deps.extend(self.readers.get(w, ()))
        deps.extend(extra)
        return deps

    def _emit(self, eng, fn, deps, inc_sem, tok):
        seq = []
        for (sname, val) in deps:
            if self.waited[eng].get(sname, 0) >= val:
                continue
            seq.append((sname, val))
            self.waited[eng][sname] = val
        sems = self.sem
        inc_val = 16 if inc_sem.startswith("q") else 1
        assert not (inc_sem.startswith("q") and inc_val != 16)

        def closure(e, seq=seq, fn=fn, inc_sem=inc_sem, inc_val=inc_val):
            for (sname, val) in seq:
                e.wait_ge(sems[sname], val)
            ins = fn(e)
            if ins is not None:
                ins.then_inc(sems[inc_sem], inc_val)
        self.ops[eng].append(closure)
        return tok

    def op(self, eng, fn, reads=(), writes=(), accum=(), extra=(), noinc=False,
           reg_reads=()):
        if noinc:
            def closure(e, fn=fn):
                fn(e)
            self.ops[eng].append(closure)
            return None
        deps = self._deps(reads, writes, extra)
        self.cnt[eng] += 1
        tok = (eng, self.cnt[eng])
        for w in writes:
            self.writer[w] = tok
            self.readers[w] = []
        for a in accum:
            self.writer[a] = tok
        for r in reads:
            self.readers.setdefault(r, []).append(tok)
        for r in reg_reads:
            self.readers.setdefault(r, []).append(tok)
        return self._emit(eng, fn, deps, eng, tok)

    def gdeps(self, keys):
        out = []
        for k in keys:
            w = self.writer.get(k)
            if w:
                out.append(w)
        return out

    def dma(self, queue, fn, reads=(), writes=(), extra=()):
        eng = "sp" if queue == "qsp" else "pool"
        deps = self._deps(reads, writes, extra)
        sname = f"{queue}{self.qidx[queue] % B.NQ[queue]}"
        self.qidx[queue] += 1
        if self.qsemcnt[sname] > 0:
            deps = deps + [(sname, self.qsemcnt[sname])]
        self.qsemcnt[sname] += 16
        tok = (sname, self.qsemcnt[sname])
        for w in writes:
            self.writer[w] = tok
            self.readers[w] = []
        for r in reads:
            self.readers.setdefault(r, []).append(tok)
        return self._emit(eng, fn, deps, sname, tok)

    def snap(self):
        toks = [(e, self.cnt[e]) for e in B.ENGS if self.cnt[e] > 0]
        toks += [(s, c) for s, c in self.qsemcnt.items() if c > 0]
        return toks

    def run(self, nc):
        with ExitStack() as ctx:
            for e in B.ENGS:
                self.sem[e] = ctx.enter_context(nc.semaphore(f"sem_{e}"))
            for s in self.qsemcnt:
                self.sem[s] = ctx.enter_context(nc.semaphore(f"sem_{s}"))
            block = ctx.enter_context(nc.Block())
            prog = self

            @block.tensor
            def _(t):
                for f in prog.ops["pe"]:
                    f(t)

            @block.scalar
            def _(s):
                for f in prog.ops["act"]:
                    f(s)

            @block.vector
            def _(v):
                for f in prog.ops["dve"]:
                    f(v)

            @block.gpsimd
            def _(g):
                for f in prog.ops["pool"]:
                    f(g)

            @block.sync
            def _(sy):
                for f in prog.ops["sp"]:
                    f(sy)
                for s, c in prog.qsemcnt.items():
                    if c > 0:
                        sy.wait_ge(prog.sem[s], c)


def build_nc():
    nc = bass.Bass("TRN2", target_bir_lowering=False)

    x_d = nc.dram_tensor("x", [S, D], F32, kind="ExternalInput")
    wq_d = nc.dram_tensor("wq", [D, D], F32, kind="ExternalInput")
    wk_d = nc.dram_tensor("wk", [D, D], F32, kind="ExternalInput")
    wv_d = nc.dram_tensor("wv", [D, D], F32, kind="ExternalInput")
    bq_d = nc.dram_tensor("bq", [1, D], F32, kind="ExternalInput")
    bk_d = nc.dram_tensor("bk", [1, D], F32, kind="ExternalInput")
    bv_d = nc.dram_tensor("bv", [1, D], F32, kind="ExternalInput")
    cx_d = nc.dram_tensor("clipx", [1, 1], F32, kind="ExternalInput")
    cd_d = nc.dram_tensor("clipd", [1, 1], F32, kind="ExternalInput")
    wg_d = nc.dram_tensor("wg", [DI, D], F32, kind="ExternalInput")
    wu_d = nc.dram_tensor("wu", [DI, D], F32, kind="ExternalInput")
    wd_d = nc.dram_tensor("wd", [D, DI], F32, kind="ExternalInput")
    out_d = nc.dram_tensor("out", [S, D], F32, kind="ExternalOutput")

    cc1i = nc.dram_tensor("cc1i", [128, 1], F32)
    cc1o = nc.dram_tensor("cc1o", [128, 1], F32)
    cc2i = nc.dram_tensor("cc2i", [128, 1], F32)
    cc2o = nc.dram_tensor("cc2o", [128, 1], F32)
    cc1bi = nc.dram_tensor("cc1bi", [128, 1], F32)
    cc1bo = nc.dram_tensor("cc1bo", [128, 1], F32)
    cc2bi = nc.dram_tensor("cc2bi", [128, 1], F32)
    cc2bo = nc.dram_tensor("cc2bo", [128, 1], F32)
    sb1_d = nc.dram_tensor("sbc1", [1, 2], F32)
    sb2_d = nc.dram_tensor("sbc2", [1, 2], F32)

    p = B()

    with ExitStack() as ctx:
        e = ctx.enter_context
        XR = e(nc.sbuf_tensor("XR", [128, NT * D], BF16))
        FM = e(nc.sbuf_tensor("FM", [128, ND * S], BF16))
        AR2 = e(nc.sbuf_tensor("AR2", [128, 49152], BF16))
        SCF = e(nc.sbuf_tensor("SCF", [128, 2816], F32))
        OST = e(nc.sbuf_tensor("OST", [128, 1024], F32))
        PTS = e(nc.sbuf_tensor("PTS", [128, 2048], BF16))
        QTS = e(nc.sbuf_tensor("QTS", [128, 2048], BF16))
        YST = e(nc.sbuf_tensor("YST", [128, 1024], BF16))
        BIA = e(nc.sbuf_tensor("BIA", [1, 3 * D], BF16))
        ONE = e(nc.sbuf_tensor("ONE", [1, 512], BF16))
        IDN = e(nc.sbuf_tensor("IDN", [128, 128], BF16))
        ST = e(nc.sbuf_tensor("ST", [128, 272], F32))
        SCL = e(nc.sbuf_tensor("SCL", [1, 64], F32))
        BNC = e(nc.sbuf_tensor("BNC", [1, 128], F32))
        BNC2 = e(nc.sbuf_tensor("BNC2", [1, 128], F32))
        WDQB = e(nc.sbuf_tensor("WDQB", [128, 2816], BF16))
        YST2 = e(nc.sbuf_tensor("YST2", [128, 1024], BF16))
        WDQC = e(nc.sbuf_tensor("WDQC", [128, 2816], BF16))

        pa = e(nc.psum_tensor("pa", [128, 2048], F32))
        pb = e(nc.psum_tensor("pb", [128, 1024], F32))
        pt = e(nc.psum_tensor("pt", [128, 1024], BF16))
        pu = e(nc.psum_tensor("pu", [128, 512], F32))

        AR2f = AR2.bitcast(F32)
        OSTb = OST.bitcast(BF16)
        YSTf = YST.bitcast(F32)
        YST2f = YST2.bitcast(F32)
        HQS = AR2f[:, 22528:24576]
        FM3 = FM.reshape([128, ND, S])
        pt3 = pt.reshape([128, 8, 128])

        C_RSTD1, C_NEGB1 = 0, 16
        C_RSTD2, C_NEGB2 = 32, 48
        C_COLMAX = 64
        C_UCOL = 80
        C_MV = 104
        C_ST6 = 106
        C_NEGM = 118
        C_RS = 119
        C_WSC = 122
        C_S1B, C_RS1B = 123, 124
        C_SDB, C_RSDB = 125, 126
        C_C32 = 127
        C_WGS = 128
        C_WUS = 150
        C_STD = 172
        C_WDS = 173
        C_M128 = 174
        C_NEGMT = 175
        C_MV2 = 176
        C_WSC2 = 210
        C_ST6B = 216
        C_MVB = 228
        C_STDB = 230
        C_EPS = 231
        C_PMX = 232
        C_RSP = 236
        C_M128B = 244
        C_LNT = 245

        L_CX, L_CD, L_GMAX, L_SIGX, L_S1, L_RS1 = 0, 1, 2, 3, 4, 5
        L_UMAX, L_SIGD, L_SD, L_RSD, L_LMAX = 6, 7, 8, 9, 10
        C_CXB, C_SGXB, C_GXB, C_LXB = 264, 266, 268, 269

        nc.const_aps.register(F32, SCALE_QK, ST[:, C_C32:C_C32 + 1])
        nc.const_aps.register(F32, EPS, ST[:, C_EPS:C_EPS + 1])
        nc.const_aps.register(F32, -0.5, ST[:, C_LNT + 17:C_LNT + 18])

        p.op("pool", lambda g: g.memset(ST[:, C_C32:C_C32 + 1], SCALE_QK),
             writes=[("st32",)])
        p.op("pool", lambda g: g.memset(ST[:, C_EPS:C_EPS + 1], EPS),
             writes=[("steps",)])
        p.op("pool", lambda g: g.memset(ST[:, C_LNT + 17:C_LNT + 18], -0.5),
             writes=[("stmh",)])
        p.op("pool", lambda g: g.memset(ONE[:], 1.0), writes=[("ONE",)])
        p.op("pool", lambda g: g.memset(IDN[:], 0.0), writes=[("IDN",)])
        p.op("pool", lambda g: g.affine_select(
            out=IDN[:], in_=IDN[:], compare_op=ALU.not_equal, fill=1.0,
            base=0, pattern=[[-1, 128]], channel_multiplier=1),
            reads=[("IDN",)], writes=[("IDN",)])

        WQv = AR2[:, 32768:40960].rearrange("p (c n) -> p c n", c=ND)
        WKVv = AR2[:, 40960:49152].rearrange("p (c n) -> p c n", c=ND)
        wq_src = wq_d.ap().rearrange("(c p) n -> p c n", p=128)
        wk_src = wk_d.ap().rearrange("(c p) n -> p c n", p=128)
        wv_src = wv_d.ap().rearrange("(c p) n -> p c n", p=128)
        XRv = XR.rearrange("p (r n) -> p r n", r=NT)
        x_src = x_d.ap().rearrange("(r p) n -> p r n", p=128)

        p.dma("qgp", lambda g: g.dma_start(out=WKVv, in_=wv_src), writes=[("WKV",)])
        p.dma("qgp", lambda g: g.dma_start(
            out=XRv[:, 0:1, :], in_=x_src[:, 0:1, :]), writes=[("XR", 0)])
        p.dma("qgp", lambda g: g.dma_start(
            out=XRv[:, 1:4, :], in_=x_src[:, 1:4, :]),
            writes=[("XR", r) for r in range(1, 4)])
        p.dma("qgp", lambda g: g.dma_start(out=SCL[0:1, L_CX:L_CX + 1], in_=cx_d.ap()),
              writes=[("scl", L_CX)])
        p.dma("qgp", lambda g: g.dma_start(out=SCL[0:1, L_CD:L_CD + 1], in_=cd_d.ap()),
              writes=[("scl", L_CD)])
        p.dma("qgp", lambda g: g.dma_start(out=BIA[0:1, 0:D], in_=bq_d.ap()),
              writes=[("bq",)])
        p.dma("qgp", lambda g: g.dma_start(out=BIA[0:1, D:2 * D], in_=bk_d.ap()),
              writes=[("bk",)])
        p.dma("qgp", lambda g: g.dma_start(out=BIA[0:1, 2 * D:3 * D], in_=bv_d.ap()),
              writes=[("bv",)])
        for i in range(1, 4):
            p.dma("qgp", lambda g, i=i: g.dma_start(
                out=XRv[:, 4 * i:4 * (i + 1), :], in_=x_src[:, 4 * i:4 * (i + 1), :]),
                writes=[("XR", r) for r in range(4 * i, 4 * i + 4)])
        p.dma("qgp", lambda g: g.dma_start(out=WQv, in_=wq_src), writes=[("WQ",)])

        p.dma("qsp", lambda sy: sy.dma_start(out=sb1_d.ap(), in_=SCL[0:1, 0:2]),
              reads=[("scl", L_CX), ("scl", L_CD)], writes=[("cbd",)])
        p.dma("qsp", lambda sy: sy.dma_start(out=ST[:, C_CXB:C_CXB + 2], in_=sb1_d.broadcast_to([128, 2])),
              reads=[("cbd",)], writes=[("cxb",)])
        p.op("act", lambda s: s.activation(out=ST[:, C_SGXB:C_SGXB + 2], in_=ST[:, C_CXB:C_CXB + 2], func=AF.Sigmoid),
             reads=[("cxb",)], writes=[("sgxb",)])

        def xr_tile(r):
            return XR[:, r * D:(r + 1) * D]

        def ln1_tile(r):
            par = r % 2
            c6 = C_ST6 if par == 0 else C_ST6B
            cmv = C_MV if par == 0 else C_MVB
            cstd = C_STD if par == 0 else C_STDB
            p.op("dve", lambda v, r=r, c6=c6: v.bn_stats(ST[:, c6:c6 + 6], xr_tile(r)[:, 0:512]),
                 reads=[("XR", r)], writes=[("st6a", par)])
            p.op("dve", lambda v, r=r, c6=c6: v.bn_stats(ST[:, c6 + 6:c6 + 12], xr_tile(r)[:, 512:1024]),
                 reads=[("XR", r)], writes=[("st6b", par)])
            p.op("dve", lambda v, c6=c6, cmv=cmv: v.bn_aggr(ST[:, cmv:cmv + 2], ST[:, c6:c6 + 12]),
                 reads=[("st6a", par), ("st6b", par)], writes=[("mv", par)])
            p.op("act", lambda s, cmv=cmv, cstd=cstd: s.activation(
                out=ST[:, cstd:cstd + 1], in_=ST[:, cmv + 1:cmv + 2], func=AF.Sqrt, bias=EPS),
                reads=[("mv", par), ("steps",)], writes=[("lstd", par)])
            p.op("dve", lambda v, r=r, cstd=cstd: v.reciprocal(ST[:, C_RSTD1 + r:C_RSTD1 + r + 1], ST[:, cstd:cstd + 1]),
                 reads=[("lstd", par)], writes=[("rstd1", r)])
            p.op("dve", lambda v, r=r, cmv=cmv: v.tensor_scalar_mul(
                ST[:, C_NEGB1 + r:C_NEGB1 + r + 1], ST[:, cmv:cmv + 1], -1.0),
                reads=[("mv", par)], writes=[("negb1", r)])
            yb = YST if r % 2 == 0 else YST2
            ky = ("YST", r % 2)
            p.op("dve", lambda v, r=r, yb=yb: v.tensor_scalar(
                yb[:], xr_tile(r), ST[:, C_NEGB1 + r:C_NEGB1 + r + 1],
                ST[:, C_RSTD1 + r:C_RSTD1 + r + 1], ALU.add, ALU.mult),
                reads=[("XR", r), ("rstd1", r), ("negb1", r)], writes=[ky])
            u = p.gdeps([ky, ("IDN",)])
            for c in range(ND):
                p.op("pe", lambda t, c=c, yb=yb: t.transpose(
                    pt[:, c * 128:(c + 1) * 128], yb[:, c * 128:(c + 1) * 128], IDN[:]),
                    writes=[("pt", c)] if c == 0 else (), extra=u if c == 0 else (),
                    noinc=(0 < c < ND - 1),
                    accum=[("pt", cc) for cc in range(1, ND)] if c == ND - 1 else (),
                    reg_reads=[ky, ("IDN",)] if c == ND - 1 else ())
            p.op("dve", lambda v, r=r: v.tensor_copy(
                FM3[:, :, r * 128:(r + 1) * 128], pt3[:, :, :]),
                reads=[("pt", 0), ("pt", ND - 1)], writes=[("FM", r)])

        KTv = AR2[:, 0:16384].rearrange("p (c n) -> p c n", c=ND)
        VTv = AR2[:, 16384:32768].rearrange("p (c n) -> p c n", c=NT)
        gctr = [0]

        def v_group(tc, h):
            bank = ("pb", h)
            keys = [("WKV",), ("bv",), ("ONE",), ("FM", tc)]
            u = p.gdeps(keys)
            p.op("pe", lambda t, tc=tc, h=h: t.matmul(
                pb[:, h * 512:(h + 1) * 512], FM3[:, 0, tc * 128:(tc + 1) * 128],
                WKVv[:, 0, h * 512:(h + 1) * 512], start=True, stop=False),
                writes=[bank], extra=u)
            for dc in range(1, ND):
                p.op("pe", lambda t, tc=tc, h=h, dc=dc: t.matmul(
                    pb[:, h * 512:(h + 1) * 512], FM3[:, dc, tc * 128:(tc + 1) * 128],
                    WKVv[:, dc, h * 512:(h + 1) * 512], start=False, stop=False), noinc=True)
            p.op("pe", lambda t, tc=tc, h=h: t.matmul(
                pb[:, h * 512:(h + 1) * 512], ONE[0:1, 0:128],
                BIA[0:1, 2 * D + h * 512:2 * D + (h + 1) * 512], start=False, stop=True),
                accum=[bank], reg_reads=keys)
            p.op("act", lambda s, tc=tc, h=h: s.activation(
                out=VTv[:, tc, h * 512:(h + 1) * 512], in_=pb[:, h * 512:(h + 1) * 512], func=AF.Copy),
                reads=[bank], writes=[("VT", tc, h)])

        def k_group(c, j):
            b = gctr[0] % 4
            gctr[0] += 1
            bank = ("pa", b)
            keys = [("WKV2",)] + [("FM", rr) for rr in range(4 * j, 4 * j + 4)]
            u = p.gdeps(keys)
            p.op("pe", lambda t, c=c, j=j, b=b: t.matmul(
                pa[:, b * 512:(b + 1) * 512], WKVv[:, 0, c * 128:(c + 1) * 128],
                FM3[:, 0, j * 512:(j + 1) * 512], start=True, stop=False),
                writes=[bank], extra=u)
            for dc in range(1, ND - 1):
                p.op("pe", lambda t, c=c, j=j, dc=dc, b=b: t.matmul(
                    pa[:, b * 512:(b + 1) * 512], WKVv[:, dc, c * 128:(c + 1) * 128],
                    FM3[:, dc, j * 512:(j + 1) * 512], start=False, stop=False), noinc=True)
            p.op("pe", lambda t, c=c, j=j, b=b: t.matmul(
                pa[:, b * 512:(b + 1) * 512], WKVv[:, ND - 1, c * 128:(c + 1) * 128],
                FM3[:, ND - 1, j * 512:(j + 1) * 512], start=False, stop=True),
                accum=[bank], reg_reads=keys)
            p.op("act", lambda s, c=c, j=j, b=b: s.activation(
                out=KTv[:, c, j * 512:(j + 1) * 512], in_=pa[:, b * 512:(b + 1) * 512], func=AF.Copy),
                reads=[bank], writes=[("KT", c, j)])

        for r in range(NT):
            ln1_tile(r)
            v_group(r, 0)
            v_group(r, 1)

        qts_banks = [(pu, 0, ("pu",)), (pb, 0, ("pb", 0)), (pb, 512, ("pb", 1))]

        def qts_buf(c):
            return (QTS, c * 512) if c < 4 else (OSTb, (c - 4) * 512)

        def emit_qts(q):
            for c in range(ND):
                bt, bo, bkey = qts_banks[c % 3]
                keys = [("WQ",), ("bq",), ("ONE",)] + [("FM", 4 * q + z) for z in range(4)]
                u = p.gdeps(keys)
                p.op("pe", lambda t, c=c, q=q, bt=bt, bo=bo: t.matmul(
                    bt[:, bo:bo + 512], WQv[:, 0, c * 128:(c + 1) * 128],
                    FM3[:, 0, q * 512:(q + 1) * 512], start=True, stop=False),
                    writes=[bkey], extra=u)
                for dc in range(1, ND):
                    p.op("pe", lambda t, c=c, q=q, dc=dc, bt=bt, bo=bo: t.matmul(
                        bt[:, bo:bo + 512], WQv[:, dc, c * 128:(c + 1) * 128],
                        FM3[:, dc, q * 512:(q + 1) * 512], start=False, stop=False), noinc=True)
                p.op("pe", lambda t, c=c, bt=bt, bo=bo: t.matmul(
                    bt[:, bo:bo + 512], BIA[0:1, c * 128:(c + 1) * 128],
                    ONE[0:1, 0:512], start=False, stop=True),
                    accum=[bkey], reg_reads=keys)
                qb, qo = qts_buf(c)
                p.op("act", lambda s, c=c, bt=bt, bo=bo, qb=qb, qo=qo: s.activation(
                    out=qb[:, qo:qo + 512], in_=bt[:, bo:bo + 512],
                    func=AF.Copy, scale=SCALE_QK),
                    reads=[bkey, ("st32",)], writes=[("QTS", c)])

        emit_qts(0)

        p.dma("qgp", lambda g: g.dma_start(out=WKVv, in_=wk_src),
              writes=[("WKV",), ("WKV2",)])
        for j in range(4):
            for c in range(ND):
                k_group(c, j)

        Psb0 = AR2[:, 40960:43008]
        Psb1 = AR2[:, 43008:45056]
        fence_psb = p.snap()

        def ln2_stats(i):
            par = i % 2
            c6 = C_ST6 if par == 0 else C_ST6B
            p.op("dve", lambda v, i=i, c6=c6: v.bn_stats(ST[:, c6:c6 + 6], xr_tile(i)[:, 0:512]),
                 reads=[("XR", i)], writes=[("st6a", par)])
            p.op("dve", lambda v, i=i, c6=c6: v.bn_stats(ST[:, c6 + 6:c6 + 12], xr_tile(i)[:, 512:1024]),
                 reads=[("XR", i)], writes=[("st6b", par)])
            p.op("dve", lambda v, i=i, c6=c6: v.bn_aggr(ST[:, C_MV2 + 2 * i:C_MV2 + 2 * i + 2], ST[:, c6:c6 + 12]),
                 reads=[("st6a", par), ("st6b", par)], writes=[("mv2", i)])
            hs = HQS[:, (i % 2) * 1024:(i % 2) * 1024 + 1024]
            p.op("dve", lambda v, i=i: v.tensor_scalar_mul(
                ST[:, C_NEGMT:C_NEGMT + 1], ST[:, C_MV2 + 2 * i:C_MV2 + 2 * i + 1], -1.0),
                reads=[("mv2", i)], writes=[("negmt",)])
            p.op("act", lambda s, i=i, hs=hs: s.activation(
                out=hs, in_=xr_tile(i), func=AF.Abs, bias=ST[:, C_NEGMT:C_NEGMT + 1]),
                reads=[("XR", i), ("negmt",)], writes=[("hqs", i % 2)])
            p.op("dve", lambda v, i=i, hs=hs: v.reduce_max(
                ST[:, C_COLMAX + i:C_COLMAX + i + 1], hs, axis=AX.X),
                reads=[("hqs", i % 2)], writes=[("colmax", i)])

        def psb(i):
            return Psb0 if i % 2 == 0 else Psb1

        def emit_scores(i):
            sub = i % 4
            def qsl(c):
                qb, qo = qts_buf(c)
                return qb[:, qo + sub * 128:qo + sub * 128 + 128]
            for j in range(4):
                bank = ("pa", j)
                keys = [("QTS", c) for c in range(ND)] + [("KT", c, j) for c in range(ND)]
                u = p.gdeps(keys)
                p.op("pe", lambda t, j=j, a=qsl(0): t.matmul(
                    pa[:, j * 512:(j + 1) * 512], a,
                    KTv[:, 0, j * 512:(j + 1) * 512], start=True, stop=False),
                    writes=[bank], extra=u)
                for c in range(1, ND - 1):
                    p.op("pe", lambda t, j=j, a=qsl(c), c=c: t.matmul(
                        pa[:, j * 512:(j + 1) * 512], a,
                        KTv[:, c, j * 512:(j + 1) * 512], start=False, stop=False), noinc=True)
                p.op("pe", lambda t, j=j, a=qsl(ND - 1): t.matmul(
                    pa[:, j * 512:(j + 1) * 512], a,
                    KTv[:, ND - 1, j * 512:(j + 1) * 512], start=False, stop=True),
                    accum=[bank], reg_reads=keys)
                p.op("dve", lambda v, j=j: v.reduce_max(
                    ST[:, C_PMX + j:C_PMX + j + 1], pa[:, j * 512:(j + 1) * 512], axis=AX.X),
                    reads=[bank], writes=[("pmx", j)])

        def emit_max(i):
            p.op("dve", lambda v: v.reduce_max(ST[:, C_NEGM:C_NEGM + 1],
                                               ST[:, C_PMX:C_PMX + 4], axis=AX.X, negate=True),
                 reads=[("pmx", j) for j in range(4)], writes=[("negm",)])

        def emit_softmax(i):
            par = i % 2
            for j in range(4):
                p.op("act", lambda s, i=i, j=j, par=par: s.activation(
                    out=psb(i)[:, j * 512:(j + 1) * 512], in_=pa[:, j * 512:(j + 1) * 512],
                    func=AF.Exp, bias=ST[:, C_NEGM:C_NEGM + 1],
                    accum_out=ST[:, C_RSP + 4 * par + j:C_RSP + 4 * par + j + 1]),
                    reads=[("pa", j), ("negm",)],
                    writes=[("Psb4", par, j), ("sumexp", par, j)], extra=fence_psb)
            p.op("dve", lambda v, par=par: v.reduce_sum(
                ST[:, C_RS + par:C_RS + par + 1], ST[:, C_RSP + 4 * par:C_RSP + 4 * par + 4], axis=AX.X),
                reads=[("sumexp", par, j) for j in range(4)], writes=[("sumt", par)])
            p.op("dve", lambda v, par=par: v.reciprocal(
                ST[:, C_RS + par:C_RS + par + 1], ST[:, C_RS + par:C_RS + par + 1]),
                reads=[("sumt", par)], writes=[("rs", par)])

        def emit_pv_head(i):
            for half in range(2):
                u = p.gdeps([("Psb4", i % 2, j) for j in range(4)] + [("IDN",)])
                for c in range(8):
                    tc = half * 8 + c
                    p.op("pe", lambda t, tc=tc, c=c, i=i: t.transpose(
                        pt[:, c * 128:(c + 1) * 128], psb(i)[:, tc * 128:(tc + 1) * 128], IDN[:]),
                        writes=[("pt", c)] if c == 0 else (), extra=u if c == 0 else (),
                        noinc=(0 < c < 7),
                        accum=[("pt", cc) for cc in range(1, 8)] if c == 7 else (),
                        reg_reads=[("Psb4", i % 2, j) for j in range(4)] + [("IDN",)] if c == 7 else ())
                p.op("act", lambda s, half=half: s.activation(
                    out=PTS[:, half * 1024:(half + 1) * 1024], in_=pt[:], func=AF.Copy),
                    reads=[("pt", 0), ("pt", 7)], writes=[("PTs", half)])
            for h in range(2):
                bank = ("pb", h)
                keys = [("PTs", 0), ("PTs", 1)] + [("VT", tc, h) for tc in range(NT)]
                u = p.gdeps(keys)
                p.op("pe", lambda t, h=h: t.matmul(
                    pb[:, h * 512:(h + 1) * 512], PTS[:, 0:128],
                    VTv[:, 0, h * 512:(h + 1) * 512], start=True, stop=False),
                    writes=[bank], extra=u)
                for tc in range(1, NT - 1):
                    p.op("pe", lambda t, h=h, tc=tc: t.matmul(
                        pb[:, h * 512:(h + 1) * 512], PTS[:, tc * 128:(tc + 1) * 128],
                        VTv[:, tc, h * 512:(h + 1) * 512], start=False, stop=False), noinc=True)
                p.op("pe", lambda t, h=h: t.matmul(
                    pb[:, h * 512:(h + 1) * 512], PTS[:, (NT - 1) * 128:NT * 128],
                    VTv[:, NT - 1, h * 512:(h + 1) * 512], start=False, stop=True),
                    accum=[bank], reg_reads=keys)
        def emit_pv_tail(i):
            p.op("dve", lambda v, i=i: v.scalar_tensor_tensor(
                xr_tile(i), pb[:], ST[:, C_RS + i % 2:C_RS + i % 2 + 1], xr_tile(i),
                ALU.mult, ALU.add),
                reads=[("pb", 0), ("pb", 1), ("rs", i % 2), ("XR", i)], writes=[("XR", i)])
            ln2_stats(i)

        def global_max_launch(cci, cco, tag, src_col, bdst, bkeys, defer_read=False,
                              send_queue="qgp"):
            p.dma(send_queue, lambda g, cci=cci, src_col=src_col: g.dma_start(
                out=cci.ap(), in_=ST[:, src_col:src_col + 1]),
                reads=[("m128", tag)], writes=[("cci", tag)])
            p.op("pool", lambda g, cci=cci, cco=cco: g.collective_compute(
                "AllReduce", ALU.max, replica_groups=[list(range(8))],
                ins=[cci.ap().opt()], outs=[cco.ap().opt()]),
                reads=[("cci", tag)], writes=[("cco", tag)])

            def emit_read():
                p.dma("qgp", lambda g, cco=cco, bdst=bdst: g.dma_start(
                    out=bdst, in_=cco.ap().rearrange("p one -> one p").broadcast_to([128, 128])),
                    reads=[("cco", tag)], writes=list(bkeys))
            if defer_read:
                return emit_read
            emit_read()

        def global_max_finish2(l_cell, bkey_a, bkey_b):
            p.op("dve", lambda v, l_cell=l_cell: v.reduce_max(
                SCL[0:1, l_cell:l_cell + 1], BNC[0:1, :], axis=AX.X),
                reads=[bkey_a], writes=[("gma", l_cell)])
            p.op("dve", lambda v, l_cell=l_cell: v.reduce_max(
                SCL[0:1, L_LMAX:L_LMAX + 1], BNC2[0:1, :], axis=AX.X),
                reads=[bkey_b], writes=[("gmb", l_cell)])
            p.op("dve", lambda v, l_cell=l_cell: v.tensor_tensor(
                SCL[0:1, l_cell:l_cell + 1], SCL[0:1, l_cell:l_cell + 1],
                SCL[0:1, L_LMAX:L_LMAX + 1], ALU.max),
                reads=[("gma", l_cell), ("gmb", l_cell)], writes=[("scl", l_cell)])

        def rstd2_batch(lo, hi, tag):
            n = hi - lo
            vv_ = ST[:, C_MV2 + 2 * lo:C_MV2 + 2 * hi].rearrange(
                "p (r two) -> p r two", two=2)[:, :, 1]
            mm_ = ST[:, C_MV2 + 2 * lo:C_MV2 + 2 * hi].rearrange(
                "p (r two) -> p r two", two=2)[:, :, 0]
            mv2k = [("mv2", i) for i in range(lo, hi)]
            p.op("act", lambda s, vv_=vv_, lo=lo, n=n: s.activation(
                out=ST[:, C_LNT + lo:C_LNT + lo + n], in_=vv_, func=AF.Ln, bias=EPS),
                reads=mv2k + [("steps",)], writes=[("lnt", tag)])
            p.op("act", lambda s, lo=lo, n=n: s.activation(
                out=ST[:, C_RSTD2 + lo:C_RSTD2 + lo + n],
                in_=ST[:, C_LNT + lo:C_LNT + lo + n], func=AF.Exp, scale=-0.5),
                reads=[("lnt", tag), ("stmh",)], writes=[("rstd2", tag)])
            p.op("dve", lambda v, mm_=mm_, lo=lo, n=n: v.tensor_scalar_mul(
                ST[:, C_NEGB2 + lo:C_NEGB2 + lo + n], mm_, -1.0),
                reads=mv2k, writes=[("negb2", tag)])
            p.op("dve", lambda v, lo=lo, n=n: v.tensor_tensor(
                ST[:, C_COLMAX + lo:C_COLMAX + lo + n], ST[:, C_COLMAX + lo:C_COLMAX + lo + n],
                ST[:, C_RSTD2 + lo:C_RSTD2 + lo + n], ALU.mult),
                reads=[("colmax", i) for i in range(lo, hi)] + [("rstd2", tag)],
                writes=[("colmaxn", tag)])

        pend = {}

        def emit_gmaxA():
            rstd2_batch(0, 13, "a")
            p.op("dve", lambda v: v.reduce_max(ST[:, C_M128:C_M128 + 1],
                                               ST[:, C_COLMAX:C_COLMAX + 13], axis=AX.X),
                 reads=[("colmaxn", "a")], writes=[("m128", "c1a")])
            pend["c1a"] = global_max_launch(cc1i, cc1o, "c1a", C_M128, YSTf[:, 0:128],
                                            [("bnca",), ("YST", 0)],
                                            defer_read=True, send_queue="qsp")

        for i in range(NT):
            if i % 4 == 0 and i > 0:
                emit_qts(i // 4)
            emit_scores(i)
            emit_max(i)
            if i > 0:
                emit_pv_head(i - 1)
            emit_softmax(i)
            if i > 0:
                emit_pv_tail(i - 1)
            if i == NT - 2:
                emit_gmaxA()
        emit_pv_head(NT - 1)
        emit_pv_tail(NT - 1)

        SG = AR2

        WINT4 = [WDQB[:, 0:1024], WDQB[:, 1024:2048],
                 QTS[:, 0:1024], QTS[:, 1024:2048]]
        WTB4 = [OSTb[:, 0:1024], OSTb[:, 1024:2048],
                PTS[:, 0:1024], PTS[:, 1024:2048]]

        STAGE4 = [SCF[:, 0:1024], SCF[:, 1024:2048],
                  HQS[:, 0:1024], HQS[:, 1024:2048]]

        def prep_quant(w_src_d, c_ws, phase, c):
                b4 = c % 4
                stage = STAGE4[b4]
                wint = WINT4[b4]
                kst = ("wstage", b4)
                kwi = ("wint", b4)
                src = w_src_d.ap()[c * 128:(c + 1) * 128, :]
                ex = FXQ if (phase == "g" and 2 <= c < 4) else ()
                p.dma("qsp", lambda sy, src=src, stage=stage: sy.dma_start(out=stage, in_=src),
                      writes=[kst], extra=ex)
                p.op("dve", lambda v, c=c, stage=stage, c_ws=c_ws: v.reduce_max(
                    ST[:, c_ws + c:c_ws + c + 1], stage, axis=AX.X, apply_absolute_value=True),
                    reads=[kst], writes=[("ws", phase, c)])
                p.op("dve", lambda v, c=c, c_ws=c_ws: v.tensor_scalar_mul(
                    ST[:, c_ws + c:c_ws + c + 1], ST[:, c_ws + c:c_ws + c + 1], INV127),
                    reads=[("ws", phase, c)], writes=[("ws", phase, c)])
                p.op("dve", lambda v, c=c, c_ws=c_ws: v.reciprocal(
                    ST[:, C_WSC:C_WSC + 1], ST[:, c_ws + c:c_ws + c + 1]),
                    reads=[("ws", phase, c)], writes=[("rws",)])
                p.op("dve", lambda v, stage=stage: v.tensor_scalar(
                    stage, stage, ST[:, C_WSC:C_WSC + 1], None, ALU.mult),
                    reads=[kst, ("rws",)], writes=[kst])
                p.op("dve", lambda v, stage=stage, wint=wint: v.tensor_scalar(
                    wint, stage, MAGIC, MAGIC, ALU.add, ALU.subtract),
                    reads=[kst], writes=[kwi],
                    extra=FXQ if (phase == "g" and 2 <= c < 4) else ())

        def prep_transp(phase, c):
                b4 = c % 4
                wint = WINT4[b4]
                wtb = WTB4[b4]
                kwi = ("wint", b4)
                kwt = ("wt", b4)
                u = p.gdeps([kwi, ("IDN",)])
                for h in range(ND):
                    p.op("pe", lambda t, h=h, wint=wint: t.transpose(
                        pt[:, h * 128:(h + 1) * 128], wint[:, h * 128:(h + 1) * 128], IDN[:]),
                        writes=[("pt", h)] if h == 0 else (), extra=u if h == 0 else (),
                        noinc=(0 < h < ND - 1),
                        accum=[("pt", hh) for hh in range(1, ND)] if h == ND - 1 else (),
                        reg_reads=[kwi, ("IDN",)] if h == ND - 1 else ())
                p.op("act", lambda s, wtb=wtb: s.activation(out=wtb, in_=pt[:], func=AF.Copy),
                     reads=[("pt", 0), ("pt", ND - 1)], writes=[kwt],
                     extra=FXQ if (phase == "g" and 2 <= c < 4) else (FA if (phase == "g" and c < 2) else ()))
        BANK6 = [("pa", 0), ("pa", 1), ("pa", 2), ("pa", 3), ("pb", 0), ("pb", 1)]

        def bank_ap(bk):
            return pa[:, bk[1] * 512:(bk[1] + 1) * 512] if bk[0] == "pa" else \
                   pb[:, bk[1] * 512:(bk[1] + 1) * 512]

        def ffn_sv(c_ws, phase, c):
                p.op("dve", lambda v, c=c, c_ws=c_ws: v.tensor_scalar(
                    ST[:, c_ws + c:c_ws + c + 1], ST[:, c_ws + c:c_ws + c + 1],
                    ST[:, C_S1B:C_S1B + 1], None, ALU.mult),
                    reads=[("ws", phase, c), ("s1b",)], writes=[("sv", phase, c)])

        def ffn_mm_group(c, j):
                b4 = c % 4
                wtb = WTB4[b4]
                kwt = ("wt", b4)
                bank = BANK6[gctr[0] % 6]
                gctr[0] += 1
                bap = bank_ap(bank)
                keys = [kwt] + [("FM", rr) for rr in range(4 * j, 4 * j + 4)]
                u = p.gdeps(keys)
                p.op("pe", lambda t, j=j, wtb=wtb, bap=bap: t.matmul(
                    bap, wtb[:, 0:128],
                    FM3[:, 0, j * 512:(j + 1) * 512], start=True, stop=False),
                    writes=[bank], extra=u)
                for h in range(1, ND - 1):
                    p.op("pe", lambda t, j=j, h=h, wtb=wtb, bap=bap: t.matmul(
                        bap, wtb[:, h * 128:(h + 1) * 128],
                        FM3[:, h, j * 512:(j + 1) * 512], start=False, stop=False), noinc=True)
                p.op("pe", lambda t, j=j, wtb=wtb, bap=bap: t.matmul(
                    bap, wtb[:, (ND - 1) * 128:ND * 128],
                    FM3[:, ND - 1, j * 512:(j + 1) * 512], start=False, stop=True),
                    accum=[bank], reg_reads=keys)
                return bank

        def ffn_mm_evac(c_ws, phase, c, j, bank, gex=None):
                bap = bank_ap(bank)
                sgt = SG[:, c * 2048 + j * 512:c * 2048 + (j + 1) * 512]
                if phase == "g":
                    p.op("act", lambda s, c=c, sgt=sgt, c_ws=c_ws, bap=bap: s.activation(
                        out=sgt, in_=bap, func=AF.Silu,
                        scale=ST[:, c_ws + c:c_ws + c + 1]),
                        reads=[bank, ("sv", phase, c)], writes=[("SG", c, j)],
                        extra=(FXQ if gex is None else gex))
                else:
                    ups = YST[:, (j % 2) * 512:(j % 2) * 512 + 512]
                    p.op("act", lambda s, c=c, ups=ups, c_ws=c_ws, bap=bap: s.activation(
                        out=ups, in_=bap, func=AF.Copy,
                        scale=ST[:, c_ws + c:c_ws + c + 1]),
                        reads=[bank, ("sv", phase, c)], writes=[("ups", j % 2)])
                    p.op("dve", lambda v, sgt=sgt, ups=ups, c=c, j=j: v.tensor_tensor(
                        sgt, sgt, ups, ALU.mult),
                        reads=[("ups", j % 2), ("SG", c, j)], writes=[("SG", c, j)])

        def ffn_mm(c_ws, phase, c, pre=()):
                if (c, "sv") not in pre:
                    ffn_sv(c_ws, phase, c)
                for j in range(4):
                    if (c, j) in pre:
                        continue
                    bank = ffn_mm_group(c, j)
                    ffn_mm_evac(c_ws, phase, c, j, bank)
                if phase == "u":
                    p.op("dve", lambda v, c=c: v.reduce_max(
                        ST[:, C_UCOL + c:C_UCOL + c + 1], SG[:, c * 2048:(c + 1) * 2048],
                        axis=AX.X, apply_absolute_value=True),
                        reads=[("SG", c, j) for j in range(4)], writes=[("ucol", c)])
                    if c == 15:
                        p.op("dve", lambda v: v.reduce_max(
                            ST[:, C_M128:C_M128 + 1], ST[:, C_UCOL:C_UCOL + 16], axis=AX.X),
                            reads=[("ucol", cc) for cc in range(16)], writes=[("m128", "c2a")])
                        global_max_launch(cc2i, cc2o, "c2a", C_M128, YST2f[:, 0:128],
                                          [("bnca2",), ("YST", 1)])

        def ffn_proj(w_src_d, c_ws, phase, skip_pre=0, tail_hook=None, pre=()):
            for c in range(skip_pre, 3):
                prep_quant(w_src_d, c_ws, phase, c)
            for c in range(skip_pre, 2):
                prep_transp(phase, c)
            for c in range(NI):
                if c + 3 < NI:
                    prep_quant(w_src_d, c_ws, phase, c + 3)
                ffn_mm(c_ws, phase, c, pre=pre)
                if c + 2 < NI:
                    prep_transp(phase, c + 2)
                if tail_hook is not None:
                    tail_hook(c)


        FA = p.snap()

        rstd2_batch(13, 16, "b")
        p.op("dve", lambda v: v.reduce_max(ST[:, C_M128B:C_M128B + 1],
                                           ST[:, C_COLMAX + 13:C_COLMAX + 16], axis=AX.X),
             reads=[("colmaxn", "b")], writes=[("m128", "c1b")])
        p.dma("qsp", lambda sy: sy.dma_start(out=cc1bi.ap(), in_=ST[:, C_M128B:C_M128B + 1]),
              reads=[("m128", "c1b")], writes=[("cci", "c1b")])
        pend["c1a"]()
        p.op("pool", lambda g: g.collective_compute(
            "AllReduce", ALU.max, replica_groups=[list(range(8))],
            ins=[cc1bi.ap().opt()], outs=[cc1bo.ap().opt()]),
            reads=[("cci", "c1b")], writes=[("cco", "c1b")])
        p.dma("qgp", lambda g: g.dma_start(
            out=YST2f[:, 0:128], in_=cc1bo.ap().rearrange("p one -> one p").broadcast_to([128, 128])),
            reads=[("cco", "c1b")], writes=[("bncb",), ("YST", 1)])
        prep_quant(wg_d, C_WGS, "g", 0)
        prep_quant(wg_d, C_WGS, "g", 1)
        prep_transp("g", 0)
        prep_transp("g", 1)

        p.op("dve", lambda v: v.reduce_max(ST[:, C_GXB:C_GXB + 1], YSTf[:, 0:128], axis=AX.X),
             reads=[("bnca",), ("YST", 0)], writes=[("gmxa",)])
        p.op("dve", lambda v: v.reduce_max(ST[:, C_LXB:C_LXB + 1], YST2f[:, 0:128], axis=AX.X),
             reads=[("bncb",), ("YST", 1)], writes=[("gmxb",)])
        p.op("dve", lambda v: v.tensor_tensor(ST[:, C_GXB:C_GXB + 1], ST[:, C_GXB:C_GXB + 1],
                                              ST[:, C_LXB:C_LXB + 1], ALU.max),
             reads=[("gmxa",), ("gmxb",)], writes=[("gmx",)])
        p.op("dve", lambda v: v.tensor_tensor(ST[:, C_S1B:C_S1B + 1], ST[:, C_GXB:C_GXB + 1],
                                              ST[:, C_SGXB:C_SGXB + 1], ALU.mult),
             reads=[("gmx",), ("sgxb",)], writes=[("s1b",)])
        p.op("dve", lambda v: v.tensor_scalar_mul(ST[:, C_S1B:C_S1B + 1], ST[:, C_S1B:C_S1B + 1], INV127),
             reads=[("s1b",)], writes=[("s1b",)])
        p.op("dve", lambda v: v.reciprocal(ST[:, C_RS1B:C_RS1B + 1], ST[:, C_S1B:C_S1B + 1]),
             reads=[("s1b",)], writes=[("rs1b",)])
        p.op("dve", lambda v: v.tensor_scalar(
            ST[:, C_RSTD2:C_RSTD2 + 16], ST[:, C_RSTD2:C_RSTD2 + 16],
            ST[:, C_RS1B:C_RS1B + 1], None, ALU.mult),
            reads=[("rstd2", "a"), ("rstd2", "b"), ("rs1b",)], writes=[("rstd2q",)])

        pre_g = []

        def pre_evac(i):
            c, j, b = pre_g[i]
            ffn_mm_evac(C_WGS, "g", c, j, b, gex=FA)

        for r in range(NT):
            hq = ("hqs", r % 2)
            hs = HQS[:, (r % 2) * 1024:(r % 2) * 1024 + 1024]
            p.op("dve", lambda v, r=r, hs=hs: v.tensor_scalar(
                hs, xr_tile(r), ST[:, C_NEGB2 + r:C_NEGB2 + r + 1],
                ST[:, C_RSTD2 + r:C_RSTD2 + r + 1], ALU.add, ALU.mult),
                reads=[("XR", r), ("negb2", "a"), ("negb2", "b"), ("rstd2q",)], writes=[hq])
            p.op("dve", lambda v, hs=hs: v.tensor_scalar(
                hs, hs, MAGIC, MAGIC, ALU.add, ALU.subtract),
                reads=[hq], writes=[hq])
            yb = YST if r % 2 == 0 else YST2
            ky = ("YST", r % 2)
            p.op("dve", lambda v, hs=hs, yb=yb: v.tensor_scalar(
                yb[:], hs, QMAX, -128.0, ALU.min, ALU.max),
                reads=[hq], writes=[ky])
            u = p.gdeps([ky, ("IDN",)])
            for c in range(ND):
                p.op("pe", lambda t, c=c, yb=yb: t.transpose(
                    pt[:, c * 128:(c + 1) * 128], yb[:, c * 128:(c + 1) * 128], IDN[:]),
                    writes=[("pt", c)] if c == 0 else (), extra=u if c == 0 else (),
                    noinc=(0 < c < ND - 1),
                    accum=[("pt", cc) for cc in range(1, ND)] if c == ND - 1 else (),
                    reg_reads=[ky, ("IDN",)] if c == ND - 1 else ())
            p.op("act", lambda s, r=r: s.activation(
                out=FM3[:, :, r * 128:(r + 1) * 128], in_=pt3[:, :, :], func=AF.Copy),
                reads=[("pt", 0), ("pt", ND - 1)], writes=[("FM", r)], extra=FA)

            if r == 3:
                ffn_sv(C_WGS, "g", 0)
                ffn_sv(C_WGS, "g", 1)
                pre_g.append((0, 0, ffn_mm_group(0, 0)))
            elif r == 5:
                pre_evac(0)
            elif r == 7:
                pre_g.append((0, 1, ffn_mm_group(0, 1)))
                pre_g.append((1, 0, ffn_mm_group(1, 0)))
            elif r == 9:
                pre_evac(1)
                pre_evac(2)
            elif r == 11:
                pre_g.append((0, 2, ffn_mm_group(0, 2)))
                pre_g.append((1, 1, ffn_mm_group(1, 1)))
            elif r == 13:
                pre_evac(3)
                pre_evac(4)

        FXQ = p.snap()

        WDT = FM
        WDTv = FM[:, 0:11264].rearrange("p (c n) -> p c n", c=NI)
        WDQ_A = FM[:, 11264:14080]
        WDQ_B = WDQB
        WDQ_C = WDQC

        def wd_quant(c, WDQ, magic_extra=(), magic_writes=()):
            stage = SCF[:, 0:2816]
            src = wd_d.ap()[c * 128:(c + 1) * 128, :]
            kq = ("wdq", c % 3)
            dw = [("wdstage",)] + ([("wstage", 0), ("wstage", 1)] if c == 0 else [])
            p.dma("qgp", lambda g, src=src, stage=stage: g.dma_start(out=stage, in_=src),
                  writes=dw)
            p.op("dve", lambda v, stage=stage: v.reduce_max(
                ST[:, C_WDS:C_WDS + 1], stage, axis=AX.X, apply_absolute_value=True),
                reads=[("wdstage",)], writes=[("wds",)])
            p.op("dve", lambda v: v.tensor_scalar_mul(ST[:, C_WDS:C_WDS + 1], ST[:, C_WDS:C_WDS + 1], INV127),
                 reads=[("wds",)], writes=[("wds",)])
            p.op("dve", lambda v: v.reciprocal(ST[:, C_WSC2:C_WSC2 + 1], ST[:, C_WDS:C_WDS + 1]),
                 reads=[("wds",)], writes=[("rws2",)])
            p.op("act", lambda s, stage=stage: s.activation(
                out=stage, in_=stage, func=AF.Copy, scale=ST[:, C_WSC2:C_WSC2 + 1]),
                reads=[("wdstage",), ("rws2",)], writes=[("wdstage",)])
            p.op("dve", lambda v, stage=stage, WDQ=WDQ: v.tensor_scalar(
                WDQ[:], stage, MAGIC, MAGIC, ALU.add, ALU.subtract),
                reads=[("wdstage",)], writes=[kq] + list(magic_writes), extra=magic_extra)
            p.op("act", lambda s, WDQ=WDQ: s.activation(
                out=WDQ[:], in_=WDQ[:], func=AF.Copy, scale=ST[:, C_WDS:C_WDS + 1]),
                reads=[kq, ("wds",)], writes=[kq])

        def wd_transp(c, WDQ, fence=()):
            kq = ("wdq", c % 3)
            sub = WDTv[:, :, (c % 4) * 128:(c % 4) * 128 + 128]
            for i0 in range(0, NI, 8):
                n = min(8, NI - i0)
                u = p.gdeps([kq, ("IDN",)]) + (list(fence) if i0 == 0 else [])
                for i in range(n):
                    p.op("pe", lambda t, i=i, i0=i0, WDQ=WDQ: t.transpose(
                        pt[:, i * 128:(i + 1) * 128], WDQ[:, (i0 + i) * 128:(i0 + i + 1) * 128], IDN[:]),
                        writes=[("pt", i)] if i == 0 else (), extra=u if i == 0 else (),
                        noinc=(0 < i < n - 1),
                        accum=[("pt", ii) for ii in range(1, n)] if i == n - 1 else (),
                        reg_reads=[kq, ("IDN",)] if i == n - 1 else ())
                p.op("dve", lambda v, i0=i0, n=n, sub=sub: v.tensor_copy(
                    sub[:, i0:i0 + n, :], pt3[:, 0:n, :]),
                    reads=[("pt", 0), ("pt", n - 1)], writes=[("WDT", c % 4)])

        def wdq_of(c):
            return [WDQ_B, WDQ_C, WDQ_A][c % 3]

        def wd_tail_hook(c):
            if c == 19:
                wd_quant(0, wdq_of(0), magic_writes=[("wint", 0), ("wint", 1)])
            elif c == 20:
                wd_quant(1, wdq_of(1))

        ffn_proj(wg_d, C_WGS, "g", skip_pre=2,
                 pre={(0, "sv"), (1, "sv"), (0, 0), (0, 1), (0, 2), (1, 0), (1, 1)})
        ffn_proj(wu_d, C_WUS, "u", tail_hook=wd_tail_hook)

        FW = p.snap()
        FWPE = [("pe", p.cnt["pe"])]

        p.op("dve", lambda v: v.reduce_max(ST[:, C_M128B:C_M128B + 1],
                                           ST[:, C_UCOL + 16:C_UCOL + NI], axis=AX.X),
             reads=[("ucol", c) for c in range(16, NI)], writes=[("m128", "c2b")])
        p.dma("qsp", lambda sy: sy.dma_start(out=cc2bi.ap(), in_=ST[:, C_M128B:C_M128B + 1]),
              reads=[("m128", "c2b")], writes=[("cci", "c2b")])
        wd_quant(2, wdq_of(2), magic_extra=FWPE)
        p.op("pool", lambda g: g.collective_compute(
            "AllReduce", ALU.max, replica_groups=[list(range(8))],
            ins=[cc2bi.ap().opt()], outs=[cc2bo.ap().opt()]),
            reads=[("cci", "c2b")], writes=[("cco", "c2b")])
        wd_transp(0, wdq_of(0), fence=FWPE)
        wd_transp(1, wdq_of(1), fence=FWPE)
        wd_transp(2, wdq_of(2), fence=FWPE)
        wd_quant(3, wdq_of(3))
        wd_transp(3, wdq_of(3), fence=FWPE)
        p.dma("qgp", lambda g: g.dma_start(
            out=YSTf[:, 0:128], in_=cc2bo.ap().rearrange("p one -> one p").broadcast_to([128, 128])),
            reads=[("cco", "c2b")], writes=[("bncb2",), ("ups", 0)])

        p.op("dve", lambda v: v.reduce_max(ST[:, C_GXB:C_GXB + 1], YST2f[:, 0:128], axis=AX.X),
             reads=[("bnca2",), ("YST", 1)], writes=[("gmxa",), ("gmx",)])
        p.op("dve", lambda v: v.reduce_max(ST[:, C_LXB:C_LXB + 1], YSTf[:, 0:128], axis=AX.X),
             reads=[("bncb2",), ("ups", 0)], writes=[("gmxb",)])
        p.op("dve", lambda v: v.tensor_tensor(ST[:, C_GXB:C_GXB + 1], ST[:, C_GXB:C_GXB + 1],
                                              ST[:, C_LXB:C_LXB + 1], ALU.max),
             reads=[("gmxa",), ("gmxb",)], writes=[("gmx",)])
        p.op("dve", lambda v: v.tensor_tensor(ST[:, C_SDB:C_SDB + 1], ST[:, C_GXB:C_GXB + 1],
                                              ST[:, C_SGXB + 1:C_SGXB + 2], ALU.mult),
             reads=[("gmx",), ("sgxb",)], writes=[("sdb",)])
        p.op("dve", lambda v: v.tensor_scalar_mul(ST[:, C_SDB:C_SDB + 1], ST[:, C_SDB:C_SDB + 1], INV127),
             reads=[("sdb",)], writes=[("sdb",)])
        p.op("dve", lambda v: v.reciprocal(ST[:, C_RSDB:C_RSDB + 1], ST[:, C_SDB:C_SDB + 1]),
             reads=[("sdb",)], writes=[("rsdb",)])

        hu = 0
        for j in range(2):
            for c in range(NI):
                par = hu % 2
                ex = FW if hu < 2 else ()
                hu += 1
                hq = ("hqs", par)
                hs = HQS[:, par * 1024:par * 1024 + 1024]
                sgs = SG[:, c * 2048 + j * 1024:c * 2048 + (j + 1) * 1024]
                rk = [("SG", c, 2 * j), ("SG", c, 2 * j + 1)]
                p.op("act", lambda s, hs=hs, sgs=sgs: s.activation(
                    out=hs, in_=sgs, func=AF.Copy, scale=ST[:, C_RSDB:C_RSDB + 1]),
                    reads=rk + [("rsdb",)], writes=[hq], extra=ex)
                p.op("dve", lambda v, hs=hs: v.tensor_scalar(
                    hs, hs, MAGIC, MAGIC, ALU.add, ALU.subtract),
                    reads=[hq], writes=[hq])
                p.op("dve", lambda v, hs=hs, sgs=sgs, rk=tuple(rk): v.tensor_scalar(
                    sgs, hs, QMAX, -128.0, ALU.min, ALU.max),
                    reads=[hq] + list(rk), writes=list(rk))

        out_v = out_d.ap().rearrange("(r p) n -> p r n", p=128)
        wdt_keys = [("WDT", q) for q in range(4)]

        def down_half(H, hook=None):
            for tb in range(NT):
                bank = BANK6[gctr[0] % 6]
                gctr[0] += 1
                bap = bank_ap(bank)
                keys = wdt_keys + [("SG", c, tb // 4) for c in range(NI)]
                trickle = (H == 0 and tb == 0)
                u = p.gdeps(wdt_keys + [("SG", 0, 0)]) if trickle else p.gdeps(keys)
                p.op("pe", lambda t, tb=tb, bap=bap: t.matmul(
                    bap,
                    SG[:, tb * 128:tb * 128 + 128],
                    WDTv[:, 0, :], start=True, stop=False),
                    writes=[bank], extra=u)
                for c in range(1, NI - 1):
                    p.op("pe", lambda t, tb=tb, c=c, bap=bap: t.matmul(
                        bap,
                        SG[:, c * 2048 + tb * 128:c * 2048 + tb * 128 + 128],
                        WDTv[:, c, :], start=False, stop=False),
                        noinc=not trickle,
                        reads=[("SG", c, 0)] if trickle else (),
                        accum=[bank] if trickle else ())
                p.op("pe", lambda t, tb=tb, bap=bap: t.matmul(
                    bap,
                    SG[:, (NI - 1) * 2048 + tb * 128:(NI - 1) * 2048 + tb * 128 + 128],
                    WDTv[:, NI - 1, :], start=False, stop=True),
                    reads=[("SG", NI - 1, 0)] if trickle else (),
                    accum=[bank], reg_reads=keys)
                xs = XR[:, tb * D + H * 512:tb * D + (H + 1) * 512]
                par = (tb + H) % 2
                ot = OST[:, par * 512:(par + 1) * 512]
                p.op("dve", lambda v, tb=tb, xs=xs, ot=ot, bap=bap: v.scalar_tensor_tensor(
                    ot, bap, ST[:, C_SDB:C_SDB + 1],
                    xs, ALU.mult, ALU.add),
                    reads=[bank, ("sdb",), ("XR", tb)], writes=[("XO", par)])
                p.dma("qsp", lambda sy, tb=tb, H=H, ot=ot: sy.dma_start(
                    out=out_v[:, tb, H * 512:(H + 1) * 512], in_=ot),
                    reads=[("XO", par)])
                if hook is not None:
                    hook(tb)

        def wd46_hook(tb):
            if tb == 0:
                wd_quant(4, wdq_of(4))
            elif tb == 3:
                wd_quant(5, wdq_of(5))
            elif tb == 7:
                wd_quant(6, wdq_of(6))

        down_half(0, hook=wd46_hook)
        wd_transp(4, wdq_of(4))
        wd_quant(7, wdq_of(7))
        wd_transp(5, wdq_of(5))
        wd_transp(6, wdq_of(6))
        wd_transp(7, wdq_of(7))
        down_half(1)

        p.run(nc)
    return nc


_NC_CACHE = {}


def kernel(L0_attn_X, L0_attn_W_q, L0_attn_W_k, L0_attn_W_v,
           L0_attn_b_q, L0_attn_b_k, L0_attn_b_v,
           L0_ffn_clip_x, L0_ffn_clip_down,
           L0_ffn_w_gate, L0_ffn_w_up, L0_ffn_w_down):
    if "nc" not in _NC_CACHE:
        _NC_CACHE["nc"] = build_nc()
    nc = _NC_CACHE["nc"]

    X = np.ascontiguousarray(np.asarray(L0_attn_X, dtype=np.float32))
    shared = {
        "wq": np.ascontiguousarray(np.asarray(L0_attn_W_q, dtype=np.float32)),
        "wk": np.ascontiguousarray(np.asarray(L0_attn_W_k, dtype=np.float32)),
        "wv": np.ascontiguousarray(np.asarray(L0_attn_W_v, dtype=np.float32)),
        "bq": np.asarray(L0_attn_b_q, dtype=np.float32).reshape(1, D),
        "bk": np.asarray(L0_attn_b_k, dtype=np.float32).reshape(1, D),
        "bv": np.asarray(L0_attn_b_v, dtype=np.float32).reshape(1, D),
        "clipx": np.asarray(L0_ffn_clip_x, dtype=np.float32).reshape(1, 1),
        "clipd": np.asarray(L0_ffn_clip_down, dtype=np.float32).reshape(1, 1),
        "wg": np.ascontiguousarray(np.asarray(L0_ffn_w_gate, dtype=np.float32)),
        "wu": np.ascontiguousarray(np.asarray(L0_ffn_w_up, dtype=np.float32)),
        "wd": np.ascontiguousarray(np.asarray(L0_ffn_w_down, dtype=np.float32)),
    }
    in_maps = [dict(shared, x=np.ascontiguousarray(X[c])) for c in range(8)]
    res = run_bass_kernel_spmd(nc, in_maps, core_ids=list(range(8)))
    out = np.stack([np.asarray(res.results[c]["out"]) for c in range(8)], axis=0)
    return out.astype(np.float32)
